# version 10
# baseline (speedup 1.0000x reference)
import numpy as np
import ml_dtypes
import concourse.bass as bass
import concourse.mybir as mybir
from concourse.bass_utils import run_bass_kernel_spmd

F32 = mybir.dt.float32
BF16 = mybir.dt.bfloat16
I32 = mybir.dt.int32
AF = mybir.ActivationFunctionType
ALU = mybir.AluOpType
AX = mybir.AxisListType
BF16NP = ml_dtypes.bfloat16

NCOR = 8
T = 8192
C = 2048
TM = T // NCOR
TCH = 512
NTC = T // TCH
HID = 8192
FCR = HID // NCOR
PJR = C // NCOR
WQ_ELS = FCR * C + PJR * HID
LAM_INIT = 0.2
RMS_EPS = 1.1920929e-07
LN_EPS = 1e-5
SHRINKS = np.linspace(0.4, 1.0, 16)
QCLIP = float(np.float32(15.499999))

CEPS_LN = 0
CEPS_RMS = 1
CINV_SQ = 2
C08 = 3
CINV_C = 4
CINV_HH = 5
CNEGH = 6
CZERO = 7

DBG = ()
PHMAX = 6
PHC_SUB = 4
PHF_SUB = 3


class SemC:
    def __init__(self, h):
        self.h = h
        self.n = 0

    def bump(self, a=1):
        self.n += a
        return self.n


def build():
    nc = bass.Bass("TRN2", target_bir_lowering=False)
    from contextlib import ExitStack
    ctx = ExitStack()

    def din(name, shape, dt):
        return nc.dram_tensor(name, shape, dt, kind="ExternalInput").ap()

    x_my = din("x_my", [TM, C], F32)
    xt_t = din("xt_t", [NTC, 128, 16 * TCH], F32)
    wcomb = din("wcomb", [128, 16 * 768], F32)
    woT_t = din("woT_t", [16, 128, 2048], F32)
    wfc_sh = din("wfc_sh", [FCR, C], F32)
    wpj_sh = din("wpj_sh", [PJR, HID], F32)
    lrows = din("lrows", [1, 1024], F32)
    ident_i = din("ident", [128, 128], BF16)
    ones_c = din("ones_c", [128, 1], BF16)
    ones_r = din("ones_r", [1, 128], BF16)
    ones_r32 = din("ones_r32", [1, 128], F32)
    csts_i = din("csts", [128, 8], F32)
    qrj_i = din("qrj", [128, 16], F32)
    qsh_i = din("qsh", [128, 512], F32)
    masks_i = din("masks", [128, 4 * 1024], BF16)
    cos_i = din("cos_t", [64, T], BF16)
    sin_i = din("sin_t", [64, T], BF16)

    out_t = nc.dram_tensor("out", [TM, C], F32, kind="ExternalOutput").ap()
    dbg_o = {}
    for name, shape, dt in [
        ("q1", [128, T], BF16), ("k1", [128, T], BF16), ("va", [128, T], BF16),
        ("q2", [128, T], BF16), ("k2", [128, T], BF16), ("vb", [128, T], BF16),
        ("y1", [128, T], BF16), ("y2", [128, T], BF16),
        ("wq", [WQ_ELS], BF16), ("yT", [128, 16 * 1024], BF16),
        ("attnT", [128, 16 * 1024], BF16), ("x1", [TM, C], F32),
        ("xn2T", [128, 16 * 1024], BF16), ("h2", [128, 64 * 512], BF16),
        ("outT", [128, 16 * 512], BF16),
        ("q1b", [128, T], BF16), ("vab", [128, T], BF16),
        ("cols", [128, 8], F32), ("rows0", [1, 4096], F32),
        ("bca", [128, 512], F32), ("bcb", [128, 512], F32),
        ("wcb", [128, 16 * 768], BF16),
        ("drow", [1, 4096], F32),
    ]:
        if name in DBG:
            dbg_o[name] = nc.dram_tensor("dbg_" + name, shape, dt, kind="ExternalOutput").ap()

    wq_pub = nc.dram_tensor("wq_pub", [WQ_ELS], BF16).ap()
    wq_all = nc.dram_tensor("wq_all", [NCOR * WQ_ELS], BF16).ap()
    y_pub = nc.dram_tensor("y_pub", [256, T], BF16).ap()
    y_all = nc.dram_tensor("y_all", [NCOR * 256, T], BF16).ap()
    x1_scr = nc.dram_tensor("x1_scr", [TM, C], F32).ap()

    SBH = {}

    def sb(name, off, shape, dt):
        h = nc.alloc_sbuf_tensor_at(name, shape, dt, offset=off)
        SBH[name] = h
        return h

    B0 = 16512
    ident = sb("c_ident", B0 + 0, [128, 128], BF16)
    onescol = sb("c_ones", B0 + 256, [128, 1], BF16)
    csts = sb("c_csts", B0 + 288, [128, 8], F32)
    onesrow = sb("c_onesr", B0 + 320, [1, 128], BF16)
    onesrow32 = sb("c_onesr32", B0 + 576, [1, 128], F32)
    lr = sb("c_lrows", B0 + 1088, [1, 1024], F32)
    lamv = sb("c_lam", B0 + 5184, [1, 16], F32)
    ecol = sb("e_cols", B0 + 5248, [128, 16], F32)
    fcol = sb("f_cols", B0 + 5248, [128, 16], F32)
    rows = sb("c_rows", B0 + 5312, [1, 8 * 512], F32)
    bca = sb("c_bca", B0 + 21696, [128, 512], F32)
    bcb = sb("c_bcb", B0 + 23744, [128, 512], F32)
    colsum = sb("c_colsum", B0 + 25792, [128, 8], F32)
    pbuf = sb("c_pbuf", B0 + 25824, [128, 4 * 1024], BF16)
    sqs = sb("c_sqs", B0 + 34016, [128, 2 * 512], BF16)
    ropes = sb("c_ropes", B0 + 36064, [128, 512], BF16)
    fixs = sb("c_fixs", B0 + 37088, [128, 512], F32)
    q_s = [sb(f"c_qs{i}", B0 + 39136 + 128 * i, [128, 32], F32) for i in range(11)]
    AR = B0 + 40576

    qw = sb("a_qw", AR, [128, 3 * 2048], F32)
    qu = sb("a_qu", AR + 24576, [128, 2048], F32)
    qt = sb("a_qt", AR + 32768, [128, 2 * 2048], F32)
    qk = sb("a_qk", AR + 49152, [128, 2048], I32)
    qd = sb("a_qd", AR + 57344, [128, 2 * 2048], F32)
    qd2 = sb("a_qd2", AR + 73728, [128, 2 * 2048], F32)
    qrj = sb("a_qrj", AR + 90112, [128, 16], F32)
    qsh = sb("a_qsh", AR + 90176, [128, 512], F32)
    qwq = sb("a_qwq", B0 + 25824, [128, 2048], BF16)

    q1 = sb("b_q1", AR + 0, [128, T], BF16)
    q2 = sb("b_q2", AR + 16384, [128, T], BF16)
    k1 = sb("b_k1", AR + 32768, [128, T], BF16)
    k2 = sb("b_k2", AR + 49152, [128, T], BF16)
    va = sb("b_va", AR + 65536, [128, T], BF16)
    vb = sb("b_vb", AR + 81920, [128, T], BF16)
    wcb = sb("b_wcb", AR + 98304, [128, 16 * 768], BF16)
    cosb = sb("b_cos", AR + 98304, [64, T], BF16)
    sinb = sb("b_sin", AR + 114688, [64, T], BF16)
    xtb = sb("b_xt", AR + 131072, [128, 2 * 8192], BF16)
    vt = sb("b_vt", AR + 131072, [128, 16384], BF16)
    y1 = sb("b_y1", AR + 65536, [128, T], BF16)
    y2 = sb("b_y2", AR + 81920, [128, T], BF16)
    masks = sb("c_masks", AR + 163840, [128, 4 * 1024], BF16)

    yT = sb("e_yT", AR + 0, [128, 16 * 1024], BF16)
    attok = sb("e_attok", AR + 0, [128, 8 * 2048], BF16)
    woTb = sb("e_woT", AR + 32768, [128, 2 * 2048], BF16)
    attnT = sb("e_attnT", AR + 40960, [128, 16 * 1024], BF16)
    exm = sb("e_xm", AR + 73728, [128, 2048], F32)
    eln = sb("e_ln", AR + 81920, [128, 2048], F32)
    ex1 = sb("e_x1", AR + 90112, [128, 2048], F32)
    esq = sb("e_sq", AR + 98304, [128, 2048], F32)
    exn2 = sb("e_xn2", AR + 106496, [128, 2048], BF16)
    xn2T = sb("e_xn2T", AR + 110592, [128, 16 * 1024], BF16)

    fwfc = sb("f_wfc", AR + 0, [128, 2 * 8192], BF16)
    fh2 = sb("f_h2", AR + 32768, [128, 64 * 512], BF16)
    fwpj = sb("f_wpj", AR + 98304, [128, 2 * 2048], BF16)
    fhr = sb("f_hr", AR + 106496, [128, 512], F32)
    fout = sb("f_out", AR + 143360, [128, 16 * 512], BF16)
    ftok = sb("f_tok", AR + 159744, [128, 2048], BF16)
    fsq = sb("f_sq", AR + 163840, [128, 2048], F32)
    fx1 = sb("f_x1", AR + 65536, [128, 2048], F32)
    fot = sb("f_ot", AR + 73728, [128, 2048], F32)
    assert AR + 172032 <= 229376

    ps_o = [ctx.enter_context(nc.psum_tensor(f"ps_o{i}", [128, 512], F32)) for i in range(4)]
    ps_s = ctx.enter_context(nc.psum_tensor("ps_s", [128, 1024], F32))
    ps_l = ctx.enter_context(nc.psum_tensor("ps_l", [128, 512], F32))
    ps_tr = ctx.enter_context(nc.psum_tensor("ps_tr", [128, 1024], BF16))

    S = {}
    for name in ["sld", "gld", "qp", "qst", "cc", "mm", "sq", "evac", "exs",
                 "vx", "pv", "expn", "vmask", "vy", "yst", "cc2", "st", "fcx",
                 "prj", "trp", "qts", "qtv"]:
        S[name] = SemC(ctx.enter_context(nc.semaphore(name)))

    prog = {k: [] for k in ["sync", "gpsimd", "vector", "scalar", "tensor"]}

    def on(eng, fn):
        prog[eng].append(fn)

    def rstd_rows(s, dst, src, eps_col):
        s.activation(rows[0:1, 3072:3584], src, AF.Ln, bias=eps_col)
        s.drain()
        s.activation(dst, rows[0:1, 3072:3584], AF.Exp, scale=csts[0:1, CNEGH:CNEGH + 1])


    def _assemble():
        with nc.Block() as block:
            @block.sync
            def _(e):
                for fn in prog["sync"]:
                    fn(e)

            @block.gpsimd
            def _(e):
                for fn in prog["gpsimd"]:
                    fn(e)

            @block.tensor
            def _(e):
                for fn in prog["tensor"]:
                    fn(e)

            @block.vector
            def _(e):
                for fn in prog["vector"]:
                    fn(e)

            @block.scalar
            def _(e):
                for fn in prog["scalar"]:
                    fn(e)
        ctx.close()
        return nc, dbg_o

    n0 = [(ident, ident_i), (onescol, ones_c), (csts, csts_i), (onesrow, ones_r),
          (onesrow32, ones_r32), (lr, lrows), (masks, masks_i), (qrj, qrj_i),
          (qsh, qsh_i)]

    def ph0b(e, n0=n0):
        for dst, src in n0:
            e.dma_start(out=dst[:], in_=src).then_inc(S["sld"].h, 16)
    on("sync", ph0b)
    S["sld"].bump(16 * 9)
    LD_CONSTS = S["sld"].n

    def qld_src(i):
        if i < 8:
            return wfc_sh.rearrange("(a p) c -> a p c", p=128)[i]
        rt, cc4 = divmod(i - 8, 4)
        return bass.AP(wpj_sh.tensor, rt * 128 * HID + cc4 * 2048,
                       [[HID, 128], [1, 2048]])

    def phAld(e):
        for i in range(16):
            if i >= 3:
                e.wait_ge(S["qp"].h, i - 2)
            e.dma_start(out=qw[:, (i % 3) * 2048:(i % 3) * 2048 + 2048],
                        in_=qld_src(i)).then_inc(S["sld"].h, 16)
    on("sync", phAld)
    QLD0 = S["sld"].n
    S["sld"].bump(16 * 16)

    def g0(e):
        e.dma_start(out=wcb[:], in_=wcomb).then_inc(S["gld"].h, 16)
        for tc in range(2):
            e.dma_start(out=xtb[:, (tc % 2) * 8192:(tc % 2) * 8192 + 8192],
                        in_=xt_t[tc]).then_inc(S["gld"].h, 16)
    on("gpsimd", g0)
    S["gld"].bump(16)
    GLD_WCB = S["gld"].n
    S["gld"].bump(32)

    def q_tnum(j):
        return j + 1 if j < 2 else 2 * j

    def q_dnum(j):
        return 3 + 2 * j if j <= 13 else 17 + j

    def qpiece(vv, p):
        wv = qw.ap()[:, (p % 3) * 2048:(p % 3) * 2048 + 2048]
        w3 = wv.rearrange("p (b k) -> p b k", k=64)
        u3 = qu.ap().rearrange("p (b k) -> p b k", k=64)
        t3 = qt.ap()[:, 0:2048].rearrange("p (b k) -> p b k", k=64)
        k3 = qk.ap().rearrange("p (b k) -> p b k", k=64)
        wq3 = qwq.ap().rearrange("p (b k) -> p b k", k=64)

        def bcx(t):
            return t.ap().unsqueeze(2).broadcast_to([128, 32, 64])
        amax, base_r, base, rbase, bEa, bEb, bcba, bcbb, E, tmp, tmp2 = q_s
        vv.tensor_reduce(amax[:], w3, AX.X, ALU.max, apply_absolute_value=True)
        vv.drain()
        vv.tensor_scalar(base_r[:], amax[:], 1e-8, None, ALU.max)
        vv.drain()
        vv.tensor_scalar(base[:], base_r[:], 1.0 / 15.0, None, ALU.mult)
        vv.drain()
        vv.reciprocal(rbase[:], base[:])
        vv.drain()
        vv.tensor_tensor(u3, w3, bcx(rbase), ALU.mult).then_inc(S["qtv"].h, 1)
        vv.drain()

        def kd(j):
            sl = (j % 2) * 2048
            vv.wait_ge(S["qts"].h, 32 * p + q_tnum(j))
            vv.tensor_scalar(qk[:], qt[:, sl:sl + 2048], QCLIP, -QCLIP,
                             ALU.min, ALU.max).then_inc(S["qtv"].h, 1)
            vv.scalar_tensor_tensor(qd[:, sl:sl + 2048], qk[:], float(SHRINKS[j]),
                                    qu[:], ALU.mult, ALU.subtract).then_inc(S["qtv"].h, 1)
        e3 = fixs.ap().rearrange("p (b j) -> p b j", j=16)
        kd(0)
        for j in range(16):
            if j < 15:
                kd(j + 1)
            sl = (j % 2) * 2048
            vv.wait_ge(S["qts"].h, 32 * p + q_dnum(j))
            vv.tensor_reduce(
                e3[:, :, j:j + 1],
                qd2.ap()[:, sl:sl + 2048].rearrange("p (b k) -> p b k", k=64),
                AX.X, ALU.add)
        vv.drain()
        vv.tensor_reduce(bEa[:], e3, AX.X, ALU.min)
        vv.drain()
        md = qd.ap()[:, 0:512].rearrange("p (b j) -> p b j", j=16)
        vv.tensor_tensor(md, e3, bEa.ap().unsqueeze(2).broadcast_to([128, 32, 16]),
                         ALU.is_equal)
        vv.drain()
        vv.tensor_tensor(qd[:, 512:1024], qd[:, 0:512], qsh[:], ALU.mult)
        vv.tensor_scalar(qd[:, 1024:1536], qd[:, 0:512], -2.0, 2.0,
                         ALU.mult, ALU.add)
        vv.drain()
        vv.tensor_tensor(qd[:, 1536:2048], qd[:, 512:1024], qd[:, 1024:1536],
                         ALU.add)
        vv.drain()
        vv.tensor_reduce(bcba[:],
                         qd.ap()[:, 1536:2048].rearrange("p (b j) -> p b j", j=16),
                         AX.X, ALU.min)
        vv.drain()
        vv.tensor_tensor(tmp[:], base[:], bcba[:], ALU.mult)
        vv.drain()
        vv.reciprocal(tmp2[:], bcba[:])
        vv.drain()
        vv.tensor_tensor(t3, u3, bcx(tmp2), ALU.mult)
        vv.tensor_scalar(qk[:], qt[:, 0:2048], QCLIP, -QCLIP, ALU.min, ALU.max)
        vv.drain()
        vv.tensor_tensor(wq3, k3, bcx(tmp), ALU.mult).then_inc(S["qp"].h, 1)

    def phAq(vv):
        for p in range(16):
            vv.wait_ge(S["sld"].h, QLD0 + 16 * (p + 1))
            if p >= 1:
                vv.wait_ge(S["qst"].h, 16 * p)
            qpiece(vv, p)
    on("vector", phAq)
    S["qp"].bump(16)
    S["qtv"].bump(33 * 16)

    def phAs(s):
        s.wait_ge(S["sld"].h, LD_CONSTS)
        for p in range(16):
            def t_of(j):
                s.activation(qt[:, (j % 2) * 2048:(j % 2) * 2048 + 2048], qu[:],
                             AF.Copy, scale=qrj[:, j:j + 1]).then_inc(S["qts"].h, 1)
            s.wait_ge(S["qtv"].h, 33 * p + 1)
            t_of(0)
            t_of(1)
            for j in range(14):
                s.wait_ge(S["qtv"].h, 33 * p + 3 + 2 * j)
                sl = (j % 2) * 2048
                s.activation(qd2[:, sl:sl + 2048], qd[:, sl:sl + 2048],
                             AF.Square).then_inc(S["qts"].h, 1)
                t_of(j + 2)
            for j in (14, 15):
                s.wait_ge(S["qtv"].h, 33 * p + 3 + 2 * j)
                sl = (j % 2) * 2048
                s.activation(qd2[:, sl:sl + 2048], qd[:, sl:sl + 2048],
                             AF.Square).then_inc(S["qts"].h, 1)
    on("scalar", phAs)
    S["qts"].bump(32 * 16)
    assert S["qp"].n == 16

    QP_OFF = [("fc", i * 128 * 2048) for i in range(8)]
    for rt in range(2):
        for cc2 in range(4):
            QP_OFF.append(("pj", FCR * C + rt * 128 * HID + cc2 * 2048))

    def phAst(e):
        for p in range(16):
            e.wait_ge(S["qp"].h, p + 1)
            kind, off = QP_OFF[p]
            if kind == "fc":
                dst = bass.AP(wq_pub.tensor, off, [[2048, 128], [1, 2048]])
            else:
                dst = bass.AP(wq_pub.tensor, off, [[HID, 128], [1, 2048]])
            e.dma_start(out=dst, in_=qwq[:]).then_inc(S["qst"].h, 16)
        e.wait_ge(S["qst"].h, 16 * 16)
        e.collective_compute(
            "AllGather", ALU.bypass,
            replica_groups=[list(range(NCOR))],
            ins=[wq_pub], outs=[wq_all],
        ).then_inc(S["cc"].h, 1)
    on("gpsimd", phAst)
    S["qst"].bump(16 * 16)
    S["cc"].bump(1)

    if "wq" in DBG:
        def dbgwq(e):
            e.wait_ge(S["qst"].h, 16 * 16)
            e.dma_start(out=dbg_o["wq"], in_=wq_pub).then_inc(S["st"].h, 16)
        on("sync", dbgwq)
        S["st"].bump(16)

    if PHMAX <= 1:
        return _assemble()

    def phBxt(e):
        for tc in range(2, NTC):
            e.wait_ge(S["evac"].h, tc - 1)
            e.dma_start(out=xtb[:, (tc % 2) * 8192:(tc % 2) * 8192 + 8192],
                        in_=xt_t[tc]).then_inc(S["gld"].h, 16)
    on("gpsimd", phBxt)
    S["gld"].bump(16 * (NTC - 2))

    def GLD_XT(tc):
        return GLD_WCB + 16 * (tc + 1)

    def phBcol(e):
        e.wait_ge(S["gld"].h, GLD_WCB)
        ins = None
        for dt in range(6):
            for ct in range(16):
                ins = e.matmul(ps_l[:, dt:dt + 1],
                               wcb[:, ct * 768 + dt * 128: ct * 768 + (dt + 1) * 128],
                               onescol[:], start=(ct == 0), stop=(ct == 15))
        ins.then_inc(S["mm"].h, 1)
    on("tensor", phBcol)
    S["mm"].bump(1)
    MM_COLS = S["mm"].n

    def phBcolv(vv):
        vv.wait_ge(S["mm"].h, MM_COLS)
        vv.tensor_copy(colsum[:, 0:6], ps_l[:, 0:6]).then_inc(S["vx"].h, 1)
    on("vector", phBcolv)
    S["vx"].bump(1)
    VX0 = S["vx"].n

    def phBsq(sx):
        for tc in range(NTC):
            xv = xtb.ap()[:, (tc % 2) * 8192:(tc % 2) * 8192 + 8192]
            sx.wait_ge(S["gld"].h, GLD_XT(tc))
            for ct in range(16):
                n = tc * 16 + ct
                if n >= 2:
                    sx.wait_ge(S["sq"].h, n - 1)
                sx.activation(sqs[:, (ct % 2) * 512:(ct % 2) * 512 + 512],
                              xv[:, ct * 512:(ct + 1) * 512], AF.Square).then_inc(S["exs"].h, 1)
            sx.wait_ge(S["vx"].h, VX0 + VX_PER_TC * tc + 1)
            sx.activation(rows[0:1, 3072:3584], rows[0:1, 2560:3072], AF.Ln,
                          bias=csts[0:1, CZERO:CZERO + 1])
            sx.drain()
            sx.activation(rows[0:1, 3584:4096], rows[0:1, 3072:3584], AF.Exp,
                          scale=csts[0:1, CNEGH:CNEGH + 1]).then_inc(S["exs"].h, 1)
    on("scalar", phBsq)
    S["exs"].bump(17 * NTC)
    EXS_SQ = S["exs"].n

    def EXS_SQN(tc, ct):
        return 17 * tc + ct + 1

    def EXS_RSTD(tc):
        return 17 * tc + 17

    QKV_PS = [ps_o[0], ps_o[1], ps_o[2], ps_o[3], None, None]

    def qkv_ps_ap(dt):
        if dt < 4:
            return ps_o[dt][:]
        return ps_s[:, (dt - 4) * 512:(dt - 4) * 512 + 512]

    VX_PER_TC = 4

    def phBmm(e):
        for tc in range(NTC):
            xv = xtb.ap()[:, (tc % 2) * 8192:(tc % 2) * 8192 + 8192]
            e.wait_ge(S["gld"].h, GLD_XT(tc))
            if tc == 0:
                e.wait_ge(S["vx"].h, VX0)
            if tc >= 1:
                e.wait_ge(S["evac"].h, tc)
            for ct in range(16):
                rhs = xv[:, ct * 512:(ct + 1) * 512]
                for dt in range(6):
                    e.matmul(qkv_ps_ap(dt),
                             wcb[:, ct * 768 + dt * 128: ct * 768 + (dt + 1) * 128],
                             rhs, start=(ct == 0), stop=(ct == 15))
                e.matmul(ps_l[0:1, :], onescol[:], rhs, start=(ct == 0), stop=(ct == 15))
                e.wait_ge(S["exs"].h, EXS_SQN(tc, ct))
                e.matmul(ps_l[32:33, :], onescol[:],
                         sqs[:, (ct % 2) * 512:(ct % 2) * 512 + 512],
                         start=(ct == 0), stop=(ct == 15)).then_inc(S["sq"].h, 1)
            e.wait_ge(S["vx"].h, VX0 + VX_PER_TC * tc + 2)
            e.matmul(ps_l[:, :], onesrow32[:], rows[0:1, 0:512],
                     start=True, stop=True).then_inc(S["mm"].h, 1)
            e.wait_ge(S["vx"].h, VX0 + VX_PER_TC * tc + 3)
            e.matmul(ps_l[:, :], onesrow32[:], rows[0:1, 512:1024],
                     start=True, stop=True).then_inc(S["mm"].h, 1)
    on("tensor", phBmm)
    S["sq"].bump(16 * NTC)
    S["mm"].bump(2 * NTC)

    def MM_BC(tc, which):
        return MM_COLS + 2 * tc + 1 + which

    dsts = [q1, q2, k1, k2, va, vb]

    def phBvec(vv):
        for tc in range(NTC):
            vv.wait_ge(S["sq"].h, 16 * (tc + 1))
            r = rows.ap()
            vv.tensor_scalar(r[0:1, 0:512], ps_l[0:1, :], 1.0 / 2048.0, None, ALU.mult)
            vv.tensor_scalar(r[0:1, 1024:1536], ps_l[32:33, :], 1.0 / 2048.0, None, ALU.mult)
            vv.drain()
            vv.tensor_tensor(r[0:1, 1536:2048], r[0:1, 0:512], r[0:1, 0:512], ALU.mult)
            vv.drain()
            vv.tensor_tensor(r[0:1, 2048:2560], r[0:1, 1024:1536], r[0:1, 1536:2048], ALU.subtract)
            vv.drain()
            vv.tensor_scalar(r[0:1, 2560:3072], r[0:1, 2048:2560], LN_EPS, None, ALU.add)
            vv.drain()
            vv.tensor_copy(r[0:1, 4 * 512:4 * 512 + 4], r[0:1, 0:4]).then_inc(S["vx"].h, 1)
            vv.wait_ge(S["exs"].h, EXS_RSTD(tc))
            vv.drain()
            vv.tensor_scalar(r[0:1, 512:1024], r[0:1, 3584:4096], -1.0, None, ALU.mult).then_inc(S["vx"].h, 1)
            vv.wait_ge(S["mm"].h, MM_BC(tc, 0))
            vv.tensor_copy(bca[:], ps_l[:, :]).then_inc(S["vx"].h, 1)
            vv.wait_ge(S["mm"].h, MM_BC(tc, 1))
            vv.tensor_copy(bcb[:], ps_l[:, :])
            vv.drain()
            ins = None
            for dt in range(6):
                vv.scalar_tensor_tensor(fixs[:], bca[:], colsum[:, dt:dt + 1],
                                        qkv_ps_ap(dt), ALU.mult, ALU.subtract)
                ins = vv.tensor_tensor(dsts[dt][:, tc * 512:(tc + 1) * 512],
                                       fixs[:], bcb[:], ALU.mult)
            ins.then_inc(S["evac"].h, 1)
            vv.tensor_copy(r[0:1, 4 * 512 + 4:4 * 512 + 8], r[0:1, 0:4]).then_inc(S["vx"].h, 1)
    on("vector", phBvec)
    S["vx"].bump(VX_PER_TC * NTC)
    S["evac"].bump(NTC)
    VX_B = S["vx"].n

    for nm, buf in [("q1b", q1), ("vab", va)]:
        if nm in DBG:
            def dfnb(e, nm=nm, buf=buf):
                e.wait_ge(S["evac"].h, NTC)
                e.dma_start(out=dbg_o[nm], in_=buf[:]).then_inc(S["st"].h, 16)
            on("sync", dfnb)
            S["st"].bump(16)
    for nm, buf in [("cols", colsum), ("bca", bca), ("bcb", bcb)]:
        if nm in DBG:
            def dfnc(e, nm=nm, buf=buf):
                e.wait_ge(S["evac"].h, NTC)
                e.dma_start(out=dbg_o[nm], in_=buf[:]).then_inc(S["st"].h, 16)
            on("sync", dfnc)
            S["st"].bump(16)
    if "wcb" in DBG:
        def dfnw(e):
            e.wait_ge(S["gld"].h, GLD_WCB)
            e.dma_start(out=dbg_o["wcb"], in_=wcb[:]).then_inc(S["st"].h, 16)
        on("sync", dfnw)
        S["st"].bump(16)
    if "rows0" in DBG:
        def dfnr(e):
            e.wait_ge(S["evac"].h, NTC)
            e.dma_start(out=dbg_o["rows0"], in_=rows[0:1, :]).then_inc(S["st"].h, 16)
        on("sync", dfnr)
        S["st"].bump(16)

    if PHMAX <= 2:
        return _assemble()

    def phCld(e):
        e.wait_ge(S["evac"].h, NTC)
        e.dma_start(out=cosb[:], in_=cos_i).then_inc(S["sld"].h, 16)
        e.dma_start(out=sinb[:], in_=sin_i).then_inc(S["sld"].h, 16)
    on("sync", phCld)
    S["sld"].bump(32)
    LD_COS = S["sld"].n

    if PHC_SUB <= 1:
        return _assemble()
    def phClamV(vv):
        vv.wait_ge(S["sld"].h, LD_CONSTS)
        r = rows.ap()
        vv.tensor_tensor(r[0:1, 0:256], lr[0:1, 0:256], lr[0:1, 256:512], ALU.mult)
        vv.drain()
        vv.tensor_reduce(r[0:1, 512:513], r[0:1, 0:256].unsqueeze(1), AX.X, ALU.add)
        vv.drain()
        vv.tensor_tensor(r[0:1, 256:512], lr[0:1, 512:768], lr[0:1, 768:1024], ALU.mult)
        vv.drain()
        vv.tensor_reduce(r[0:1, 513:514], r[0:1, 256:512].unsqueeze(1), AX.X, ALU.add)
        vv.drain()
        vv.tensor_copy(r[0:1, 514:516], r[0:1, 512:514]).then_inc(S["vx"].h, 1)
    on("vector", phClamV)
    S["vx"].bump(1)
    VX_LAM = S["vx"].n

    def phClamS(s):
        s.wait_ge(S["vx"].h, VX_LAM)
        s.activation(rows[0:1, 516:517], rows[0:1, 514:515], AF.Exp)
        s.activation(rows[0:1, 517:518], rows[0:1, 515:516], AF.Exp).then_inc(S["exs"].h, 1)
    on("scalar", phClamS)
    S["exs"].bump(1)
    EXS_LAM = S["exs"].n

    def phClamV2(vv):
        vv.wait_ge(S["exs"].h, EXS_LAM)
        vv.drain()
        r = rows.ap()
        vv.tensor_tensor(r[0:1, 518:519], r[0:1, 516:517], r[0:1, 517:518], ALU.subtract)
        vv.drain()
        vv.tensor_scalar(lamv[0:1, 0:1], r[0:1, 518:519], LAM_INIT, None, ALU.add)
        vv.drain()
        vv.tensor_scalar(lamv[0:1, 1:2], lamv[0:1, 0:1], -1.0, None, ALU.mult)
        vv.drain()
        if "drow" in DBG:
            vv.tensor_copy(lamv[0:1, 4:8], lr[0:1, 0:4])
            vv.tensor_copy(lamv[0:1, 8:15], r[0:1, 512:519])
            vv.drain()
        vv.tensor_copy(lamv[0:1, 2:3], lamv[0:1, 0:1]).then_inc(S["vx"].h, 1)
    on("vector", phClamV2)
    S["vx"].bump(1)
    VX_LAM2 = S["vx"].n

    if PHC_SUB <= 2:
        return _assemble()
    RR0_sq = S["exs"].n
    RR0_mm = S["mm"].n
    RR0_vx = S["vx"].n
    RR0_rs = None
    taus = [q1, q2, k1, k2]

    def phCsq(sx):
        n = 0
        for ti in range(4):
            for tc in range(NTC):
                if n == 0:
                    sx.wait_ge(S["evac"].h, NTC)
                if n >= 2:
                    sx.wait_ge(S["sq"].h, 16 * NTC + n - 1)
                sx.activation(sqs[:, (n % 2) * 512:(n % 2) * 512 + 512],
                              taus[ti][:, tc * 512:(tc + 1) * 512], AF.Square).then_inc(S["exs"].h, 1)
                sx.wait_ge(S["vx"].h, RR0_vx + 2 * n + 1)
                sx.activation(rows[0:1, 3072:3584], rows[0:1, 2560:3072], AF.Ln,
                              bias=csts[0:1, CEPS_RMS:CEPS_RMS + 1])
                sx.drain()
                sx.activation(rows[0:1, 3584:4096], rows[0:1, 3072:3584], AF.Exp,
                              scale=csts[0:1, CNEGH:CNEGH + 1]).then_inc(S["exs"].h, 1)
                n += 1
    on("scalar", phCsq)
    S["exs"].bump(2 * 64)
    RR_SQ_END = S["exs"].n

    def phCmm(e):
        n = 0
        for ti in range(4):
            for tc in range(NTC):
                if n >= 1:
                    e.wait_ge(S["vx"].h, RR0_vx + 2 * (n - 1) + 2)
                e.wait_ge(S["exs"].h, RR0_sq + 2 * n + 1)
                e.matmul(ps_l[0:1, :], onescol[:],
                         sqs[:, (n % 2) * 512:(n % 2) * 512 + 512],
                         start=True, stop=True).then_inc(S["sq"].h, 1)
                e.wait_ge(S["vx"].h, RR0_vx + 2 * n + 1)
                e.wait_ge(S["exs"].h, RR0_sq + 2 * n + 2)
                e.matmul(ps_l[:, :], onesrow32[:], rows[0:1, 3584:4096],
                         start=True, stop=True).then_inc(S["mm"].h, 1)
                n += 1
    on("tensor", phCmm)
    S["sq"].bump(64)
    S["mm"].bump(64)

    def phCrow(vv):
        n = 0
        for ti in range(4):
            for tc in range(NTC):
                vv.wait_ge(S["sq"].h, 16 * NTC + n + 1)
                r = rows.ap()
                vv.tensor_scalar(r[0:1, 2560:3072], ps_l[0:1, :], 1.0 / 128.0, None, ALU.mult)
                vv.drain()
                vv.tensor_copy(r[0:1, 520:524], r[0:1, 0:4]).then_inc(S["vx"].h, 1)
                vv.wait_ge(S["mm"].h, RR0_mm + n + 1)
                if n == 0:
                    vv.wait_ge(S["sld"].h, LD_COS)
                cs = cosb.ap()[0:64, tc * 512:(tc + 1) * 512]
                sn = sinb.ap()[0:64, tc * 512:(tc + 1) * 512]
                x1v = taus[ti].ap()[0:64, tc * 512:(tc + 1) * 512]
                x2v = taus[ti].ap()[64:128, tc * 512:(tc + 1) * 512]
                A = pbuf.ap()[0:64, 2048:2560]
                Bs = pbuf.ap()[0:64, 2560:3072]
                Cs = pbuf.ap()[0:64, 3072:3584]
                X2 = pbuf.ap()[0:64, 3584:4096]
                vv.tensor_copy(X2, x2v)
                vv.tensor_tensor(A, x1v, cs, ALU.mult)
                vv.tensor_tensor(Bs, X2, sn, ALU.mult)
                vv.tensor_tensor(Cs, A, Bs, ALU.add)
                vv.tensor_tensor(A, X2, cs, ALU.mult)
                vv.tensor_tensor(Bs, x1v, sn, ALU.mult)
                vv.tensor_tensor(X2, A, Bs, ALU.subtract)
                vv.drain()
                vv.tensor_tensor(x1v, Cs, ps_l[0:64, :], ALU.mult)
                vv.tensor_tensor(A, X2, ps_l[0:64, :], ALU.mult)
                vv.drain()
                vv.tensor_copy(x2v, A).then_inc(S["vx"].h, 1)
                n += 1
    on("vector", phCrow)
    S["vx"].bump(2 * 64)
    VX_RR_END = S["vx"].n

    if PHC_SUB <= 3:
        return _assemble()
    TRP0 = S["mm"].n
    VTC0 = S["vx"].n

    def phCvt(e):
        for kt in range(64):
            if kt >= 1:
                e.wait_ge(S["vx"].h, VTC0 + kt)
            e.transpose(ps_tr[:, 0:128], va[:, kt * 128:(kt + 1) * 128], ident[:])
            e.transpose(ps_tr[:, 128:256], vb[:, kt * 128:(kt + 1) * 128],
                        ident[:]).then_inc(S["mm"].h, 1)
    on("tensor", phCvt)
    S["mm"].bump(64)

    def phCvtv(vv):
        for kt in range(64):
            vv.wait_ge(S["mm"].h, TRP0 + kt + 1)
            vv.tensor_copy(vt[:, kt * 256:kt * 256 + 256],
                           ps_tr[:, 0:256]).then_inc(S["vx"].h, 1)
    on("vector", phCvtv)
    S["vx"].bump(64)
    VX_VT_END = S["vx"].n

    for nm, buf in [("q1", q1), ("q2", q2), ("k1", k1), ("k2", k2), ("va", va), ("vb", vb)]:
        if nm in DBG:
            def dfn(e, nm=nm, buf=buf):
                e.wait_ge(S["vx"].h, VX_VT_END)
                e.dma_start(out=dbg_o[nm], in_=buf[:]).then_inc(S["st"].h, 16)
            on("sync", dfn)
            S["st"].bump(16)

    if PHMAX <= 3:
        return _assemble()

    NKT = [4 * (c + 1) for c in range(NTC)]
    GOF = [0]
    for c in range(NTC):
        GOF.append(GOF[-1] + NKT[c])
    NSTEP = GOF[-1]

    EXP0 = S["expn"].n
    PV0 = S["pv"].n
    MM_D0 = S["mm"].n
    VMASK0 = S["vmask"].n
    VX_D0 = S["vx"].n

    def is_diag(c, kt):
        return kt >= NKT[c] - 4

    dmask_count = {}
    nmask = 0
    for c in range(NTC):
        for kt in range(NKT[c]):
            if is_diag(c, kt):
                nmask += 1
                dmask_count[(c, kt)] = nmask

    def pv_group(e, c, kt, first, last):
        g = GOF[c] + kt
        if is_diag(c, kt):
            e.wait_ge(S["vmask"].h, VMASK0 + dmask_count[(c, kt)])
            slot = 2 + (g % 2)
        else:
            slot = g % 2
        pv = pbuf.ap()[:, slot * 1024:(slot + 1) * 1024]
        st_, sp = first, last
        e.matmul(ps_o[0][:], vt[:, kt * 256:kt * 256 + 128], pv[:, 0:512], start=st_, stop=sp)
        e.matmul(ps_o[1][:], vt[:, kt * 256 + 128:kt * 256 + 256], pv[:, 0:512], start=st_, stop=sp)
        e.matmul(ps_o[2][:], vt[:, kt * 256:kt * 256 + 128], pv[:, 512:1024], start=st_, stop=sp)
        e.matmul(ps_o[3][:], vt[:, kt * 256 + 128:kt * 256 + 256], pv[:, 512:1024], start=st_, stop=sp)
        e.matmul(ps_l[0:1, :], onescol[:], pv[:, 0:512], start=st_, stop=sp)
        ins = e.matmul(ps_l[32:33, :], onescol[:], pv[:, 512:1024], start=st_, stop=sp)
        return ins

    def phDt(e):
        e.wait_ge(S["vx"].h, VX_VT_END)
        for c in range(NTC):
            e.wait_ge(S["vy"].h, c)
            qs1 = q1.ap()[:, c * 512:(c + 1) * 512]
            qs2 = q2.ap()[:, c * 512:(c + 1) * 512]
            for kt in range(NKT[c]):
                g = GOF[c] + kt
                e.wait_ge(S["expn"].h, EXP0 + g)
                e.matmul(ps_s[:, 0:512], k1[:, kt * 128:(kt + 1) * 128], qs1,
                         start=True, stop=True)
                e.matmul(ps_s[:, 512:1024], k2[:, kt * 128:(kt + 1) * 128], qs2,
                         start=True, stop=True).then_inc(S["mm"].h, 1)
                if kt >= 1:
                    ins = pv_group(e, c, kt - 1, kt - 1 == 0, kt - 1 == NKT[c] - 1)
                    if kt - 1 == NKT[c] - 1:
                        ins.then_inc(S["pv"].h, 1)
            e.wait_ge(S["expn"].h, EXP0 + GOF[c + 1])
            ins = pv_group(e, c, NKT[c] - 1, NKT[c] - 1 == 0, True)
            ins.then_inc(S["pv"].h, 1)
            e.wait_ge(S["vx"].h, VX_D0 + 2 * c + 1)
            e.matmul(ps_s[:, 0:512], onesrow32[:], rows[0:1, 1024:1536],
                     start=True, stop=True)
            e.matmul(ps_s[:, 512:1024], onesrow32[:], rows[0:1, 1536:2048],
                     start=True, stop=True).then_inc(S["mm"].h, 1)
    on("tensor", phDt)
    for c in range(NTC):
        S["mm"].bump(NKT[c])
        S["pv"].bump(1)
        S["mm"].bump(1)

    def MM_S2(c, kt):
        base = MM_D0
        for cc2 in range(c):
            base += NKT[cc2] + 1
        return base + kt + 1

    def MM_DBC(c):
        base = MM_D0
        for cc2 in range(c):
            base += NKT[cc2] + 1
        return base + NKT[c] + 1

    def phDs(s):
        for c in range(NTC):
            for kt in range(NKT[c]):
                g = GOF[c] + kt
                s.wait_ge(S["mm"].h, MM_S2(c, kt))
                s.activation(pbuf[:, (g % 2) * 1024:(g % 2) * 1024 + 1024],
                             ps_s[:, :], AF.Exp,
                             scale=csts[:, CINV_SQ:CINV_SQ + 1]).then_inc(S["expn"].h, 1)
    on("scalar", phDs)
    S["expn"].bump(NSTEP)

    def phDcomb(vv):
        for c in range(NTC):
            for kt in range(NKT[c]):
                if not is_diag(c, kt):
                    continue
                g = GOF[c] + kt
                d = kt - 4 * c
                vv.wait_ge(S["expn"].h, EXP0 + g + 1)
                vv.tensor_tensor(pbuf[:, (2 + g % 2) * 1024:(2 + g % 2) * 1024 + 1024],
                                 pbuf[:, (g % 2) * 1024:(g % 2) * 1024 + 1024],
                                 masks[:, d * 1024:(d + 1) * 1024],
                                 ALU.mult).then_inc(S["vmask"].h, 1)
            vv.wait_ge(S["pv"].h, PV0 + c + 1)
            r = rows.ap()
            vv.tensor_copy(r[0:1, 0:512], ps_l[0:1, :])
            vv.tensor_copy(r[0:1, 512:1024], ps_l[32:33, :])
            vv.drain()
            vv.reciprocal(r[0:1, 1024:1536], r[0:1, 0:512])
            vv.drain()
            vv.reciprocal(r[0:1, 2048:2560], r[0:1, 512:1024])
            vv.drain()
            vv.tensor_scalar(r[0:1, 1536:2048], r[0:1, 2048:2560],
                             lamv[0:1, 1:2], None, ALU.mult)
            vv.drain()
            if c == 10 and "drow" in DBG:
                vv.tensor_copy(r[0:1, 2560:3072], r[0:1, 512:1024])
                vv.tensor_copy(r[0:1, 3072:3584], r[0:1, 1536:2048])
                vv.tensor_copy(r[0:1, 3584:3600], lamv[0:1, 0:16])
                vv.tensor_copy(r[0:1, 3600:3612], r[0:1, 0:12])
                vv.drain()
            vv.tensor_copy(r[0:1, 528:532], r[0:1, 0:4]).then_inc(S["vx"].h, 1)
            vv.wait_ge(S["mm"].h, MM_DBC(c))
            vv.tensor_copy(bca[:], ps_s[:, 0:512])
            vv.tensor_copy(bcb[:], ps_s[:, 512:1024])
            vv.drain()
            pbt = pbuf.ap().bitcast(F32)[:, 1024:1536]
            ins = None
            for half in range(2):
                vv.tensor_tensor(fixs[:], ps_o[0 + half][:], bca[:], ALU.mult)
                vv.tensor_tensor(pbt, ps_o[2 + half][:], bcb[:], ALU.mult)
                ydst = (y1 if half == 0 else y2).ap()[:, c * 512:(c + 1) * 512]
                ins = vv.tensor_tensor(ydst, fixs[:], pbt, ALU.add)
            ins.then_inc(S["vy"].h, 1)
            vv.tensor_copy(r[0:1, 532:536], r[0:1, 0:4]).then_inc(S["vx"].h, 1)
    on("vector", phDcomb)
    S["vmask"].bump(nmask)
    S["vx"].bump(2 * NTC)
    S["vy"].bump(NTC)

    def phDyst(e):
        for c in range(NTC):
            e.wait_ge(S["vy"].h, c + 1)
            e.dma_start(out=y_pub[0:128, c * 512:(c + 1) * 512],
                        in_=y1[:, c * 512:(c + 1) * 512]).then_inc(S["yst"].h, 16)
            e.dma_start(out=y_pub[128:256, c * 512:(c + 1) * 512],
                        in_=y2[:, c * 512:(c + 1) * 512]).then_inc(S["yst"].h, 16)
    on("sync", phDyst)
    S["yst"].bump(32 * NTC)

    for nm, buf in [("y1", y1), ("y2", y2)]:
        if nm in DBG:
            def dfn(e, nm=nm, buf=buf):
                e.wait_ge(S["vy"].h, NTC)
                e.dma_start(out=dbg_o[nm], in_=buf[:]).then_inc(S["st"].h, 16)
            on("sync", dfn)
            S["st"].bump(16)
    if "drow" in DBG:
        def dfnd(e):
            e.wait_ge(S["vy"].h, NTC)
            e.dma_start(out=dbg_o["drow"], in_=rows[0:1, :]).then_inc(S["st"].h, 16)
        on("sync", dfnd)
        S["st"].bump(16)

    def phDag(e):
        e.wait_ge(S["yst"].h, 32 * NTC)
        e.collective_compute(
            "AllGather", ALU.bypass,
            replica_groups=[list(range(NCOR))],
            ins=[y_pub], outs=[y_all],
        ).then_inc(S["cc2"].h, 1)
    on("gpsimd", phDag)
    S["cc2"].bump(1)

    if PHMAX <= 4:
        return _assemble()

    WOD0 = S["yst"].n
    YT0 = S["qst"].n

    def phEyt(e):
        e.wait_ge(S["cc2"].h, 1)
        for arm in e.switch_core_id(n=128):
            rr = arm.logical % NCOR
            for g in range(16):
                src = bass.AP(y_all.tensor, (g * 128) * T + rr * TM, [[T, 128], [1, TM]])
                e.dma_start(out=yT[:, g * 1024:(g + 1) * 1024], in_=src).then_inc(S["qst"].h, 16)
    on("sync", phEyt)
    S["qst"].bump(16 * 16)
    YT_END = S["qst"].n

    GLD_WO0 = S["gld"].n

    def phEwo_ld(e):
        e.wait_ge(S["vy"].h, NTC)
        for ct2 in range(16):
            if ct2 >= 2:
                e.wait_ge(S["fcx"].h, ct2 - 1)
            e.dma_start(out=woTb[:, (ct2 % 2) * 2048:(ct2 % 2) * 2048 + 2048],
                        in_=woT_t[ct2]).then_inc(S["gld"].h, 16)
    on("gpsimd", phEwo_ld)
    S["gld"].bump(16 * 16)

    def GLD_WO(ct2):
        return GLD_WO0 + 16 * (ct2 + 1)

    EXS_WO0 = S["exs"].n

    def phEwo_mm(e):
        e.wait_ge(S["qst"].h, YT_END)
        for ct2 in range(16):
            e.wait_ge(S["gld"].h, GLD_WO(ct2))
            if ct2 >= 1:
                e.wait_ge(S["exs"].h, EXS_WO0 + ct2)
            wv = woTb.ap()[:, (ct2 % 2) * 2048:(ct2 % 2) * 2048 + 2048]
            ins = None
            for ci in range(16):
                for hf in range(2):
                    ins = e.matmul(ps_s[:, hf * 512:(hf + 1) * 512],
                                   wv[:, ci * 128:(ci + 1) * 128],
                                   yT[:, ci * 1024 + hf * 512: ci * 1024 + (hf + 1) * 512],
                                   start=(ci == 0), stop=(ci == 15))
            ins.then_inc(S["fcx"].h, 1)
    on("tensor", phEwo_mm)
    S["fcx"].bump(16)
    FCX_WO = S["fcx"].n

    def phEwo_ev(s):
        for ct2 in range(16):
            s.wait_ge(S["fcx"].h, ct2 + 1)
            s.activation(attnT[:, ct2 * 1024:(ct2 + 1) * 1024], ps_s[:, :],
                         AF.Copy, scale=csts[:, C08:C08 + 1]).then_inc(S["exs"].h, 1)
    on("scalar", phEwo_ev)
    S["exs"].bump(16)
    EXS_WO_END = S["exs"].n

    if "attnT" in DBG:
        def dfn(e):
            e.wait_ge(S["exs"].h, EXS_WO_END)
            e.dma_start(out=dbg_o["attnT"], in_=attnT[:]).then_inc(S["st"].h, 16)
        on("sync", dfn)
        S["st"].bump(16)

    TRE0 = S["mm"].n
    VTE0 = S["vx"].n

    def phEtr(e):
        e.wait_ge(S["exs"].h, EXS_WO_END)
        e.wait_ge(S["fcx"].h, FCX_WO)
        n = 0
        for tt in range(8):
            for ct2 in range(16):
                if n >= 1:
                    e.wait_ge(S["vx"].h, VTE0 + n)
                e.transpose(ps_tr[:, 0:128],
                            attnT[:, ct2 * 1024 + tt * 128: ct2 * 1024 + (tt + 1) * 128],
                            ident[:]).then_inc(S["mm"].h, 1)
                n += 1
    on("tensor", phEtr)
    S["mm"].bump(128)

    def phEtrv(vv):
        n = 0
        for tt in range(8):
            for ct2 in range(16):
                vv.wait_ge(S["mm"].h, TRE0 + n + 1)
                vv.tensor_copy(attok[:, tt * 2048 + ct2 * 128: tt * 2048 + (ct2 + 1) * 128],
                               ps_tr[:, 0:128]).then_inc(S["vx"].h, 1)
                n += 1
    on("vector", phEtrv)
    S["vx"].bump(128)
    VX_TRE_END = S["vx"].n

    def ln_tokmajor(tag, src_ap, src_sq_f32, dst_ap, cols, eps_col, sgate_vx=None):
        v0 = S["vx"].n
        s0 = S["exs"].n

        def vpart1(vv):
            vv.tensor_reduce(cols[:, 0:1], src_ap, AX.X, ALU.add)
            vv.drain()
            vv.tensor_scalar(cols[:, 2:3], cols[:, 0:1], 1.0 / 2048.0, None, ALU.mult)
            vv.drain()
            vv.tensor_copy(cols[:, 8:9], cols[:, 2:3]).then_inc(S["vx"].h, 1)
        on("vector", vpart1)
        S["vx"].bump(1)

        def spart1(s, sgate_vx=sgate_vx):
            if sgate_vx is not None:
                sem, val = sgate_vx if isinstance(sgate_vx, tuple) else (S["vx"], sgate_vx)
                s.wait_ge(sem.h, val)
            s.activation(src_sq_f32, src_ap, AF.Square).then_inc(S["exs"].h, 1)
        on("scalar", spart1)
        S["exs"].bump(1)

        def vpart2(vv):
            vv.wait_ge(S["exs"].h, s0 + 1)
            vv.tensor_reduce(cols[:, 1:2], src_sq_f32, AX.X, ALU.add)
            vv.drain()
            vv.tensor_scalar(cols[:, 3:4], cols[:, 1:2], 1.0 / 2048.0, None, ALU.mult)
            vv.drain()
            vv.tensor_tensor(cols[:, 4:5], cols[:, 2:3], cols[:, 2:3], ALU.mult)
            vv.drain()
            vv.tensor_tensor(cols[:, 5:6], cols[:, 3:4], cols[:, 4:5], ALU.subtract)
            vv.drain()
            vv.tensor_copy(cols[:, 9:10], cols[:, 5:6]).then_inc(S["vx"].h, 1)
        on("vector", vpart2)
        S["vx"].bump(1)

        def spart2(s):
            s.wait_ge(S["vx"].h, v0 + 2)
            s.activation(cols[:, 6:7], cols[:, 5:6], AF.Ln, bias=eps_col)
            s.drain()
            s.activation(cols[:, 7:8], cols[:, 6:7], AF.Exp,
                         scale=csts[:, CNEGH:CNEGH + 1]).then_inc(S["exs"].h, 1)
        on("scalar", spart2)
        S["exs"].bump(1)

        def vpart3(vv):
            vv.wait_ge(S["exs"].h, s0 + 2)
            vv.drain()
            vv.tensor_tensor(cols[:, 10:11], cols[:, 2:3], cols[:, 7:8], ALU.mult)
            vv.drain()
            vv.tensor_scalar(cols[:, 11:12], cols[:, 10:11], -1.0, None, ALU.mult)
            vv.drain()
            vv.tensor_copy(cols[:, 12:13], cols[:, 11:12]).then_inc(S["vx"].h, 1)
        on("vector", vpart3)
        S["vx"].bump(1)

        def vpart4(vv):
            vv.tensor_scalar(dst_ap, src_ap, cols[:, 7:8], cols[:, 11:12],
                             ALU.mult, ALU.add).then_inc(S["vx"].h, 1)
        on("vector", vpart4)
        S["vx"].bump(1)
        return S["vx"].n, S["exs"].n

    XM0 = S["sld"].n
    TRX0 = S["mm"].n

    def phEln(tt):
        def ld(e, tt=tt):
            if tt >= 1:
                e.wait_ge(S["vx"].h, 0)
            e.wait_ge(S["vy"].h, 0)
            e.dma_start(out=exm[:], in_=x_my.rearrange("(a p) c -> a p c", p=128)[tt]).then_inc(S["sld"].h, 16)
        v_res_prev = S["vx"].n
        def ld2(e, tt=tt, v_res_prev=v_res_prev):
            e.wait_ge(S["vx"].h, VX_TRE_END if tt == 0 else v_res_prev)
            e.dma_start(out=exm[:], in_=x_my.rearrange("(a p) c -> a p c", p=128)[tt]).then_inc(S["sld"].h, 16)
        on("sync", ld2)
        S["sld"].bump(16)
        ld_n = S["sld"].n

        src = attok.ap()[:, tt * 2048:(tt + 1) * 2048]
        def vgate(vv, tt=tt):
            if tt == 0:
                vv.wait_ge(S["vx"].h, VX_TRE_END)
        on("vector", vgate)
        ln_tokmajor("ln2", src, esq[:], eln[:], ecol, csts[:, CEPS_LN:CEPS_LN + 1],
                    sgate_vx=VTE0 + (tt + 1) * 16)
        v_apply2 = S["vx"].n

        def vres(vv, ld_n=ld_n):
            vv.wait_ge(S["sld"].h, ld_n)
            vv.drain()
            vv.tensor_tensor(ex1[:], exm[:], eln[:], ALU.add).then_inc(S["vx"].h, 1)
        on("vector", vres)
        S["vx"].bump(1)
        v_res = S["vx"].n

        def st1(e, tt=tt, v_res=v_res):
            e.wait_ge(S["vx"].h, v_res)
            e.dma_start(out=x1_scr.rearrange("(a p) c -> a p c", p=128)[tt],
                        in_=ex1[:]).then_inc(S["qst"].h, 16)
        on("sync", st1)
        S["qst"].bump(16)

        if "x1" in DBG:
            def dx1(e, tt=tt, v_res=v_res):
                e.wait_ge(S["vx"].h, v_res)
                e.dma_start(out=dbg_o["x1"].rearrange("(a p) c -> a p c", p=128)[tt],
                            in_=ex1[:]).then_inc(S["st"].h, 16)
            on("sync", dx1)
            S["st"].bump(16)

        def vgate3(vv, v_res=v_res):
            vv.wait_ge(S["vx"].h, v_res)
        on("vector", vgate3)

        def sgate3(s, v_res=v_res):
            s.wait_ge(S["vx"].h, v_res)
        on("scalar", sgate3)
        ln_tokmajor("ln3", ex1.ap()[:, :], esq[:], exn2[:],
                    ecol, csts[:, CEPS_LN:CEPS_LN + 1], sgate_vx=None)
        s_apply3 = S["exs"].n

        n0m = S["mm"].n
        n0v = S["vx"].n

        def etr(e, tt=tt, s_apply3=s_apply3, n0v=n0v):
            e.wait_ge(S["exs"].h, s_apply3)
            e.wait_ge(S["vx"].h, n0v)
            for ci in range(16):
                if ci >= 1:
                    e.wait_ge(S["vx"].h, n0v + ci)
                e.transpose(ps_tr[:, 0:128],
                            exn2[:, ci * 128:(ci + 1) * 128], ident[:]).then_inc(S["mm"].h, 1)
        on("tensor", etr)
        S["mm"].bump(16)

        def etrv(vv, tt=tt, n0m=n0m):
            for ci in range(16):
                vv.wait_ge(S["mm"].h, n0m + ci + 1)
                vv.tensor_copy(xn2T[:, ci * 1024 + tt * 128: ci * 1024 + (tt + 1) * 128],
                               ps_tr[:, 0:128]).then_inc(S["vx"].h, 1)
        on("vector", etrv)
        S["vx"].bump(16)

    for tt in range(8):
        phEln(tt)
    VX_E_END = S["vx"].n
    X1ST_END = S["qst"].n

    if "xn2T" in DBG:
        def dxn(e):
            e.wait_ge(S["vx"].h, VX_E_END)
            e.dma_start(out=dbg_o["xn2T"], in_=xn2T[:]).then_inc(S["st"].h, 16)
        on("sync", dxn)
        S["st"].bump(16)

    if PHMAX <= 5:
        return _assemble()

    def wfcq_rows(g0, nrows, c0, ncols):
        rk = g0 // FCR
        off = rk * WQ_ELS + (g0 % FCR) * C + c0
        return bass.AP(wq_all.tensor, off, [[C, nrows], [1, ncols]])

    def wpjq_rows(p0, nrows, h0, ncols):
        rk = p0 // PJR
        off = rk * WQ_ELS + FCR * C + (p0 % PJR) * HID + h0
        return bass.AP(wq_all.tensor, off, [[HID, nrows], [1, ncols]])

    FCX_WO = S["fcx"].n

    for hh in range(2):
        FWF0 = S["qst"].n
        FCX0 = S["fcx"].n

        def phFfcld(e, hh=hh, FCX0=FCX0):
            if hh == 0:
                e.wait_ge(S["cc"].h, 1)
            for hg in range(16):
                if hg >= 2:
                    e.wait_ge(S["fcx"].h, FCX0 + 4 * (hg - 1))
                for ci in range(16):
                    e.dma_start_transpose(
                        fwfc[:, (hg % 2) * 8192 + ci * 512:(hg % 2) * 8192 + (ci + 1) * 512],
                        wfcq_rows(hg * 512, 512, ci * 128, 128)).then_inc(S["qst"].h, 16)
        on("sync", phFfcld)
        S["qst"].bump(16 * 16 * 16)

        def FWF(hg, FWF0=FWF0):
            return FWF0 + 16 * 16 * (hg + 1)

        VH0 = S["vx"].n
        EXH0 = S["exs"].n

        def phFfc(e, hh=hh, FWF=FWF, VH0=VH0, FCX0=FCX0):
            if hh == 0:
                e.wait_ge(S["vx"].h, VX_E_END)
            for hg in range(16):
                e.wait_ge(S["qst"].h, FWF(hg))
                for h4 in range(4):
                    ht = hg * 4 + h4
                    if ht >= 2:
                        e.wait_ge(S["vx"].h, VH0 + ht - 1)
                    wv = fwfc.ap()[:, (hg % 2) * 8192:(hg % 2) * 8192 + 8192]
                    ins = None
                    for ci in range(16):
                        ins = e.matmul(ps_o[ht % 2][:],
                                       wv[:, ci * 512 + h4 * 128: ci * 512 + (h4 + 1) * 128],
                                       xn2T[:, ci * 1024 + hh * 512: ci * 1024 + (hh + 1) * 512],
                                       start=(ci == 0), stop=(ci == 15))
                    ins.then_inc(S["fcx"].h, 1)
        on("tensor", phFfc)
        S["fcx"].bump(64)

        def phFrelu(vv, hh=hh, VH0=VH0, EXH0=EXH0, FCX0=FCX0):
            for ht in range(64):
                vv.wait_ge(S["fcx"].h, FCX0 + ht + 1)
                if ht >= 1:
                    vv.wait_ge(S["exs"].h, EXH0 + ht)
                vv.tensor_scalar(fhr[:], ps_o[ht % 2][:], 0.0, None, ALU.max).then_inc(S["vx"].h, 1)
        on("vector", phFrelu)
        S["vx"].bump(64)

        def phFsqr(s, hh=hh, VH0=VH0):
            for ht in range(64):
                s.wait_ge(S["vx"].h, VH0 + ht + 1)
                s.activation(fh2[:, ht * 512:(ht + 1) * 512], fhr[:], AF.Square).then_inc(S["exs"].h, 1)
        on("scalar", phFsqr)
        S["exs"].bump(64)
        EXH_END = S["exs"].n

        if "h2" in DBG and hh == 0:
            def dh2(e, EXH_END=EXH_END):
                e.wait_ge(S["exs"].h, EXH_END)
                e.dma_start(out=dbg_o["h2"], in_=fh2[:]).then_inc(S["st"].h, 16)
            on("sync", dh2)
            S["st"].bump(16)

        if PHF_SUB <= 1:
            continue
        def pj_ps(i):
            if i < 2:
                return ps_o[2 + i][:]
            return ps_s[:, (i - 2) * 512:(i - 2) * 512 + 512]

        PJLD0 = S["qst"].n
        PRJ0 = S["prj"].n
        EXPJ0 = S["exs"].n

        def phFpjld(e, hh=hh, PJLD0=PJLD0, PRJ0=PRJ0):
            for cg in range(4):
                for ht in range(64):
                    n = cg * 64 + ht
                    if n >= 2:
                        e.wait_ge(S["prj"].h, PRJ0 + n - 1)
                    for rr2 in range(2):
                        e.dma_start_transpose(
                            fwpj[:, (n % 2) * 2048 + rr2 * 256:(n % 2) * 2048 + rr2 * 256 + 256],
                            wpjq_rows(cg * 512 + rr2 * 256, 256,
                                      ht * 128, 128)).then_inc(S["qst"].h, 16)
        on("sync", phFpjld)
        S["qst"].bump(32 * 256)

        def PJLD(n, PJLD0=PJLD0):
            return PJLD0 + 32 * (n + 1)

        def phFpj(e, hh=hh, PJLD=PJLD, EXH_END=EXH_END, EXPJ0=EXPJ0, PRJ0=PRJ0):
            e.wait_ge(S["exs"].h, EXH_END)
            for cg in range(4):
                if cg >= 1:
                    e.wait_ge(S["exs"].h, EXPJ0 + 4 * cg)
                ins = None
                for ht in range(64):
                    n = cg * 64 + ht
                    e.wait_ge(S["qst"].h, PJLD(n))
                    wv = fwpj.ap()[:, (n % 2) * 2048:(n % 2) * 2048 + 512]
                    for c4 in range(4):
                        ins = e.matmul(pj_ps(c4), wv[:, c4 * 128:(c4 + 1) * 128],
                                       fh2[:, ht * 512:(ht + 1) * 512],
                                       start=(ht == 0), stop=(ht == 63))
                    ins.then_inc(S["prj"].h, 1)
        on("tensor", phFpj)
        S["prj"].bump(256)

        def phFpjev(s, hh=hh, PRJ0=PRJ0):
            for cg in range(4):
                s.wait_ge(S["prj"].h, PRJ0 + (cg + 1) * 64)
                for c4 in range(4):
                    s.activation(fout[:, (cg * 4 + c4) * 512:(cg * 4 + c4 + 1) * 512],
                                 pj_ps(c4), AF.Copy).then_inc(S["exs"].h, 1)
        on("scalar", phFpjev)
        S["exs"].bump(16)
        EXPJ_END = S["exs"].n

        if "outT" in DBG and hh == 0:
            def dot(e, EXPJ_END=EXPJ_END):
                e.wait_ge(S["exs"].h, EXPJ_END)
                e.dma_start(out=dbg_o["outT"], in_=fout[:]).then_inc(S["st"].h, 16)
            on("sync", dot)
            S["st"].bump(16)

        if PHF_SUB <= 2:
            continue
        s_cast = EXPJ_END

        n0m = S["mm"].n
        VT0 = S["trp"].n

        def ftr(e, hh=hh, s_cast=s_cast, VT0=VT0):
            e.wait_ge(S["exs"].h, s_cast)
            n = 0
            for t4 in range(4):
                for ci in range(16):
                    if n >= 1:
                        e.wait_ge(S["trp"].h, VT0 + n)
                    e.transpose(ps_tr[:, 0:128],
                                fout[:, ci * 512 + t4 * 128: ci * 512 + (t4 + 1) * 128],
                                ident[:]).then_inc(S["mm"].h, 1)
                    n += 1
        on("tensor", ftr)
        S["mm"].bump(64)

        for t4 in range(4):
            n0 = n0m + t4 * 16

            def ftrv(vv, t4=t4, n0=n0):
                for ci in range(16):
                    vv.wait_ge(S["mm"].h, n0 + ci + 1)
                    vv.tensor_copy(ftok[:, ci * 128:(ci + 1) * 128],
                                   ps_tr[:, 0:128]).then_inc(S["trp"].h, 1)
            on("vector", ftrv)
            S["trp"].bump(16)
            vv_n = S["trp"].n

            gtt = hh * 4 + t4

            def fx1ld(e, gtt=gtt, vv_n=vv_n):
                e.wait_ge(S["trp"].h, vv_n)
                e.dma_start(out=fx1[:], in_=x1_scr.rearrange("(a p) c -> a p c", p=128)[gtt]).then_inc(S["sld"].h, 16)
            on("gpsimd", fx1ld)
            S["sld"].bump(16)
            ldn = S["sld"].n

            ln_tokmajor("ln4", ftok.ap()[:, :], fsq[:], fot[:], fcol,
                        csts[:, CEPS_LN:CEPS_LN + 1], sgate_vx=(S["trp"], vv_n))

            def fres3(vv, ldn=ldn):
                vv.wait_ge(S["sld"].h, ldn)
                vv.drain()
                vv.tensor_tensor(eln[:], fx1[:], fot[:], ALU.add).then_inc(S["vx"].h, 1)
            on("vector", fres3)
            S["vx"].bump(1)
            vres_n = S["vx"].n

            def fst(e, gtt=gtt, vres_n=vres_n):
                e.wait_ge(S["vx"].h, vres_n)
                e.dma_start(out=out_t.rearrange("(a p) c -> a p c", p=128)[gtt],
                            in_=eln[:]).then_inc(S["st"].h, 16)
            on("gpsimd", fst)
            S["st"].bump(16)
            stn = S["st"].n

            def egate(vv, stn=stn):
                vv.wait_ge(S["st"].h, stn)
            on("vector", egate)

    ST_END = S["st"].n

    def fin(e):
        e.wait_ge(S["st"].h, ST_END)
    on("sync", fin)

    return _assemble()


def _host_inputs(inputs):
    x = np.ascontiguousarray(np.asarray(inputs["L0_attn_x"], dtype=np.float32).reshape(T, C))
    wq = np.asarray(inputs["L0_attn_wq"], dtype=np.float32)
    wk = np.asarray(inputs["L0_attn_wk"], dtype=np.float32)
    wv = np.asarray(inputs["L0_attn_wv"], dtype=np.float32)
    wo = np.asarray(inputs["L0_attn_wo"], dtype=np.float32)
    wfc = np.asarray(inputs["L0_ffn_w_fc"], dtype=np.float32)
    wpj = np.asarray(inputs["L0_ffn_w_proj"], dtype=np.float32)

    xT = np.ascontiguousarray(x.T)
    xt_t = np.ascontiguousarray(
        xT.reshape(16, 128, NTC, TCH).transpose(2, 1, 0, 3)).reshape(NTC, 128, 16 * TCH)
    woT = np.ascontiguousarray(wo.T)
    woT_t = np.ascontiguousarray(
        woT.reshape(16, 128, 16, 128).transpose(2, 1, 0, 3)).reshape(16, 128, 2048)

    lrows = np.concatenate([
        np.asarray(inputs["L0_attn_lq1"], dtype=np.float32).ravel(),
        np.asarray(inputs["L0_attn_lk1"], dtype=np.float32).ravel(),
        np.asarray(inputs["L0_attn_lq2"], dtype=np.float32).ravel(),
        np.asarray(inputs["L0_attn_lk2"], dtype=np.float32).ravel()])[None, :]

    ident = np.eye(128, dtype=BF16NP)
    ones_c = np.ones((128, 1), dtype=BF16NP)
    ones_r = np.ones((1, 128), dtype=BF16NP)
    csts = np.zeros((128, 8), dtype=np.float32)
    csts[:, CEPS_LN] = LN_EPS
    csts[:, CEPS_RMS] = RMS_EPS
    csts[:, CINV_SQ] = 1.0 / np.sqrt(128.0)
    csts[:, C08] = 1.0 - LAM_INIT
    csts[:, CINV_C] = 1.0 / C
    csts[:, CINV_HH] = 1.0 / 128.0
    csts[:, CNEGH] = -0.5
    csts[:, CZERO] = 0.0

    masks = np.zeros((128, 4, 2, 512), dtype=BF16NP)
    qq = np.arange(512)[None, :]
    tk = np.arange(128)[:, None]
    for d in range(4):
        m = (tk + d * 128 <= qq).astype(np.float32)
        masks[:, d, 0, :] = m
        masks[:, d, 1, :] = m
    masks = np.ascontiguousarray(masks.reshape(128, 4096))

    invf = (1.0 / 10000.0) ** (np.arange(0, 128, 2, dtype=np.float32) / 128.0)

    in_maps = []
    for r in range(NCOR):
        wcomb_r = np.concatenate([
            np.ascontiguousarray(wq[r * 256:(r + 1) * 256, :].T),
            np.ascontiguousarray(wk[r * 256:(r + 1) * 256, :].T),
            np.ascontiguousarray(wv[r * 256:(r + 1) * 256, :].T)], axis=1)
        wcomb_t = np.ascontiguousarray(
            wcomb_r.reshape(16, 128, 768)).transpose(1, 0, 2).reshape(128, 16 * 768)
        wcomb_t = np.ascontiguousarray(wcomb_t)
        ang = (invf * np.float32(r)).astype(np.float32)
        cos_t = np.ascontiguousarray(np.tile(np.cos(ang).astype(BF16NP)[:, None], (1, T)))
        sin_t = np.ascontiguousarray(np.tile(np.sin(ang).astype(BF16NP)[:, None], (1, T)))
        in_maps.append(dict(
            x_my=np.ascontiguousarray(x[r * TM:(r + 1) * TM, :]),
            xt_t=xt_t, wcomb=wcomb_t, woT_t=woT_t,
            wfc_sh=np.ascontiguousarray(wfc[r * FCR:(r + 1) * FCR, :]),
            wpj_sh=np.ascontiguousarray(wpj[r * PJR:(r + 1) * PJR, :]),
            lrows=lrows, ident=ident, ones_c=ones_c, ones_r=ones_r,
            csts=csts, masks=masks, cos_t=cos_t, sin_t=sin_t,
            ones_r32=np.ones((1, 128), dtype=np.float32),
            qrj=np.tile((1.0 / SHRINKS.astype(np.float32)).astype(np.float32)[None, :],
                        (128, 1)),
            qsh=np.tile(SHRINKS.astype(np.float32), (128, 32 // 16 * 16)).reshape(128, 512),
        ))
    return in_maps


def kernel(**inputs):
    in_maps = _host_inputs(inputs)
    nc, _ = build()
    res = run_bass_kernel_spmd(nc, in_maps, core_ids=list(range(NCOR)))
    out = np.concatenate([np.asarray(res.results[r]["out"]) for r in range(NCOR)], axis=0)
    return out.reshape(1, T, C).astype(np.float32)


# version 11
# speedup vs baseline: 1.1595x; 1.1595x over previous
import numpy as np
import ml_dtypes
import concourse.bass as bass
import concourse.mybir as mybir
from concourse.bass_utils import run_bass_kernel_spmd

F32 = mybir.dt.float32
BF16 = mybir.dt.bfloat16
I32 = mybir.dt.int32
AF = mybir.ActivationFunctionType
ALU = mybir.AluOpType
AX = mybir.AxisListType
BF16NP = ml_dtypes.bfloat16

NCOR = 8
T = 8192
C = 2048
TM = T // NCOR
TCH = 512
NTC = T // TCH
HID = 8192
FCR = HID // NCOR
PJR = C // NCOR
WQ_ELS = FCR * C + PJR * HID
LAM_INIT = 0.2
RMS_EPS = 1.1920929e-07
LN_EPS = 1e-5
SHRINKS = np.linspace(0.4, 1.0, 16)
QCLIP = float(np.float32(15.499999))

CEPS_LN = 0
CEPS_RMS = 1
CINV_SQ = 2
C08 = 3
CINV_C = 4
CINV_HH = 5
CNEGH = 6
CZERO = 7

DBG = ()
PHMAX = 6
PHC_SUB = 4
PHF_SUB = 3


class SemC:
    def __init__(self, h):
        self.h = h
        self.n = 0

    def bump(self, a=1):
        self.n += a
        return self.n


def build():
    nc = bass.Bass("TRN2", target_bir_lowering=False)
    from contextlib import ExitStack
    ctx = ExitStack()

    def din(name, shape, dt):
        return nc.dram_tensor(name, shape, dt, kind="ExternalInput").ap()

    x_my = din("x_my", [TM, C], F32)
    xt_t = din("xt_t", [NTC, 128, 16 * TCH], F32)
    wcomb = din("wcomb", [128, 16 * 768], F32)
    woT_t = din("woT_t", [16, 128, 2048], F32)
    wfc_sh = din("wfc_sh", [FCR, C], F32)
    wpj_sh = din("wpj_sh", [PJR, HID], F32)
    lrows = din("lrows", [1, 1024], F32)
    ident_i = din("ident", [128, 128], BF16)
    ones_c = din("ones_c", [128, 1], BF16)
    ones_r = din("ones_r", [1, 128], BF16)
    ones_r32 = din("ones_r32", [1, 128], F32)
    csts_i = din("csts", [128, 8], F32)
    qrj_i = din("qrj", [128, 16], F32)
    qsh_i = din("qsh", [128, 512], F32)
    masks_i = din("masks", [128, 4 * 1024], BF16)
    cos_i = din("cos_t", [64, T], BF16)
    sin_i = din("sin_t", [64, T], BF16)

    out_t = nc.dram_tensor("out", [TM, C], F32, kind="ExternalOutput").ap()
    dbg_o = {}
    for name, shape, dt in [
        ("q1", [128, T], BF16), ("k1", [128, T], BF16), ("va", [128, T], BF16),
        ("q2", [128, T], BF16), ("k2", [128, T], BF16), ("vb", [128, T], BF16),
        ("y1", [128, T], BF16), ("y2", [128, T], BF16),
        ("wq", [WQ_ELS], BF16), ("yT", [128, 16 * 1024], BF16),
        ("attnT", [128, 16 * 1024], BF16), ("x1", [TM, C], F32),
        ("xn2T", [128, 16 * 1024], BF16), ("h2", [128, 64 * 512], BF16),
        ("outT", [128, 16 * 512], BF16),
        ("q1b", [128, T], BF16), ("vab", [128, T], BF16),
        ("cols", [128, 8], F32), ("rows0", [1, 4096], F32),
        ("bca", [128, 512], F32), ("bcb", [128, 512], F32),
        ("wcb", [128, 16 * 768], BF16),
        ("drow", [1, 4096], F32),
    ]:
        if name in DBG:
            dbg_o[name] = nc.dram_tensor("dbg_" + name, shape, dt, kind="ExternalOutput").ap()

    wq_pub = nc.dram_tensor("wq_pub", [WQ_ELS], BF16).ap()
    wq_all = nc.dram_tensor("wq_all", [NCOR * WQ_ELS], BF16).ap()
    y_pub = nc.dram_tensor("y_pub", [256, T], BF16).ap()
    y_all = nc.dram_tensor("y_all", [NCOR * 256, T], BF16).ap()
    x1_scr = nc.dram_tensor("x1_scr", [TM, C], F32).ap()

    SBH = {}

    def sb(name, off, shape, dt):
        h = nc.alloc_sbuf_tensor_at(name, shape, dt, offset=off)
        SBH[name] = h
        return h

    B0 = 16512
    ident = sb("c_ident", B0 + 0, [128, 128], BF16)
    onescol = sb("c_ones", B0 + 256, [128, 1], BF16)
    csts = sb("c_csts", B0 + 288, [128, 8], F32)
    onesrow = sb("c_onesr", B0 + 320, [1, 128], BF16)
    onesrow32 = sb("c_onesr32", B0 + 576, [1, 128], F32)
    lr = sb("c_lrows", B0 + 1088, [1, 1024], F32)
    lamv = sb("c_lam", B0 + 5184, [1, 16], F32)
    ecol = sb("e_cols", B0 + 5248, [128, 16], F32)
    fcol = sb("f_cols", B0 + 5248, [128, 16], F32)
    rows = sb("c_rows", B0 + 5312, [1, 8 * 512], F32)
    bca = sb("c_bca", B0 + 21696, [128, 512], F32)
    bcb = sb("c_bcb", B0 + 23744, [128, 512], F32)
    colsum = sb("c_colsum", B0 + 25792, [128, 8], F32)
    pbuf = sb("c_pbuf", B0 + 25824, [128, 4 * 1024], BF16)
    sqs = sb("c_sqs", B0 + 34016, [128, 2 * 512], BF16)
    ropes = sb("c_ropes", B0 + 36064, [128, 512], BF16)
    fixs = sb("c_fixs", B0 + 37088, [128, 512], F32)
    q_s = [sb(f"c_qs{i}", B0 + 39136 + 128 * i, [128, 32], F32) for i in range(11)]
    AR = B0 + 40576

    qw = sb("a_qw", AR, [128, 3 * 2048], F32)
    qu = sb("a_qu", AR + 24576, [128, 2048], F32)
    qt = sb("a_qt", AR + 32768, [128, 2 * 2048], F32)
    qk = sb("a_qk", AR + 49152, [128, 2048], I32)
    qd = sb("a_qd", AR + 57344, [128, 2 * 2048], F32)
    qd2 = sb("a_qd2", AR + 73728, [128, 2 * 2048], F32)
    qrj = sb("a_qrj", AR + 90112, [128, 16], F32)
    qsh = sb("a_qsh", AR + 90176, [128, 512], F32)
    qtt = sb("a_qtt", AR + 92224, [128, 256], BF16)
    qwq = sb("a_qwq", B0 + 25824, [128, 2048], BF16)

    q1 = sb("b_q1", AR + 0, [128, T], BF16)
    q2 = sb("b_q2", AR + 16384, [128, T], BF16)
    k1 = sb("b_k1", AR + 32768, [128, T], BF16)
    k2 = sb("b_k2", AR + 49152, [128, T], BF16)
    va = sb("b_va", AR + 65536, [128, T], BF16)
    vb = sb("b_vb", AR + 81920, [128, T], BF16)
    wcb = sb("b_wcb", AR + 98304, [128, 16 * 768], BF16)
    cosb = sb("b_cos", AR + 98304, [64, T], BF16)
    sinb = sb("b_sin", AR + 114688, [64, T], BF16)
    xtb = sb("b_xt", AR + 131072, [128, 2 * 8192], BF16)
    vt = sb("b_vt", AR + 131072, [128, 16384], BF16)
    y1 = sb("b_y1", AR + 65536, [128, T], BF16)
    y2 = sb("b_y2", AR + 81920, [128, T], BF16)
    masks = sb("c_masks", AR + 163840, [128, 4 * 1024], BF16)

    yT = sb("e_yT", AR + 0, [128, 16 * 1024], BF16)
    attok = sb("e_attok", AR + 0, [128, 8 * 2048], BF16)
    woTb = sb("e_woT", AR + 32768, [128, 2 * 2048], BF16)
    attnT = sb("e_attnT", AR + 40960, [128, 16 * 1024], BF16)
    exm = sb("e_xm", AR + 73728, [128, 2048], F32)
    eln = sb("e_ln", AR + 81920, [128, 2048], F32)
    ex1 = sb("e_x1", AR + 90112, [128, 2048], F32)
    esq = sb("e_sq", AR + 98304, [128, 2048], F32)
    exn2 = sb("e_xn2", AR + 106496, [128, 2048], BF16)
    xn2T = sb("e_xn2T", AR + 110592, [128, 16 * 1024], BF16)

    fwfc = sb("f_wfc", AR + 0, [128, 2 * 8192], BF16)
    fh2 = sb("f_h2", AR + 32768, [128, 64 * 512], BF16)
    fwpj = sb("f_wpj", AR + 98304, [128, 2 * 2048], BF16)
    fhr = sb("f_hr", AR + 106496, [128, 512], F32)
    fout = sb("f_out", AR + 143360, [128, 16 * 512], BF16)
    ftok = sb("f_tok", AR + 159744, [128, 2048], BF16)
    fsq = sb("f_sq", AR + 163840, [128, 2048], F32)
    fx1 = sb("f_x1", AR + 65536, [128, 2048], F32)
    fot = sb("f_ot", AR + 73728, [128, 2048], F32)
    assert AR + 172032 <= 229376

    ps_o = [ctx.enter_context(nc.psum_tensor(f"ps_o{i}", [128, 512], F32)) for i in range(4)]
    ps_s = ctx.enter_context(nc.psum_tensor("ps_s", [128, 1024], F32))
    ps_l = ctx.enter_context(nc.psum_tensor("ps_l", [128, 512], F32))
    ps_tr = ctx.enter_context(nc.psum_tensor("ps_tr", [128, 1024], BF16))

    S = {}
    for name in ["sld", "gld", "qp", "qst", "cc", "mm", "sq", "evac", "exs",
                 "vx", "pv", "expn", "vmask", "vy", "yst", "cc2", "st", "fcx",
                 "prj", "trp", "qts", "qtv"]:
        S[name] = SemC(ctx.enter_context(nc.semaphore(name)))

    prog = {k: [] for k in ["sync", "gpsimd", "vector", "scalar", "tensor"]}

    def on(eng, fn):
        prog[eng].append(fn)

    def rstd_rows(s, dst, src, eps_col):
        s.activation(rows[0:1, 3072:3584], src, AF.Ln, bias=eps_col)
        s.drain()
        s.activation(dst, rows[0:1, 3072:3584], AF.Exp, scale=csts[0:1, CNEGH:CNEGH + 1])


    def _assemble():
        with nc.Block() as block:
            @block.sync
            def _(e):
                for fn in prog["sync"]:
                    fn(e)

            @block.gpsimd
            def _(e):
                for fn in prog["gpsimd"]:
                    fn(e)

            @block.tensor
            def _(e):
                for fn in prog["tensor"]:
                    fn(e)

            @block.vector
            def _(e):
                for fn in prog["vector"]:
                    fn(e)

            @block.scalar
            def _(e):
                for fn in prog["scalar"]:
                    fn(e)
        ctx.close()
        return nc, dbg_o

    n0 = [(ident, ident_i), (onescol, ones_c), (csts, csts_i), (onesrow, ones_r),
          (onesrow32, ones_r32), (lr, lrows), (masks, masks_i), (qrj, qrj_i),
          (qsh, qsh_i)]

    def ph0b(e, n0=n0):
        for dst, src in n0:
            e.dma_start(out=dst[:], in_=src).then_inc(S["sld"].h, 16)
    on("sync", ph0b)
    S["sld"].bump(16 * 9)
    LD_CONSTS = S["sld"].n

    def qld_src(i):
        if i < 8:
            return wfc_sh.rearrange("(a p) c -> a p c", p=128)[i]
        rt, cc4 = divmod(i - 8, 4)
        return bass.AP(wpj_sh.tensor, rt * 128 * HID + cc4 * 2048,
                       [[HID, 128], [1, 2048]])

    def phAld(e):
        for i in range(16):
            if i >= 3:
                e.wait_ge(S["qp"].h, i - 2)
            e.dma_start(out=qw[:, (i % 3) * 2048:(i % 3) * 2048 + 2048],
                        in_=qld_src(i)).then_inc(S["sld"].h, 16)
    on("sync", phAld)
    QLD0 = S["sld"].n
    S["sld"].bump(16 * 16)

    def g0(e):
        e.dma_start(out=wcb[:], in_=wcomb).then_inc(S["gld"].h, 16)
        for tc in range(2):
            e.dma_start(out=xtb[:, (tc % 2) * 8192:(tc % 2) * 8192 + 8192],
                        in_=xt_t[tc]).then_inc(S["gld"].h, 16)
    on("gpsimd", g0)
    S["gld"].bump(16)
    GLD_WCB = S["gld"].n
    S["gld"].bump(32)

    def q_tnum(j):
        return j + 1 if j < 2 else 2 * j

    def q_dnum(j):
        return 3 + 2 * j if j <= 13 else 17 + j

    def qpiece(vv, p):
        wv = qw.ap()[:, (p % 3) * 2048:(p % 3) * 2048 + 2048]
        w3 = wv.rearrange("p (b k) -> p b k", k=64)
        u3 = qu.ap().rearrange("p (b k) -> p b k", k=64)
        t3 = qt.ap()[:, 0:2048].rearrange("p (b k) -> p b k", k=64)
        k3 = qk.ap().rearrange("p (b k) -> p b k", k=64)
        wq3 = qwq.ap().rearrange("p (b k) -> p b k", k=64)

        def bcx(t):
            return t.ap().unsqueeze(2).broadcast_to([128, 32, 64])
        amax, base_r, base, rbase, bEa, bEb, bcba, bcbb, E, tmp, tmp2 = q_s
        vv.tensor_reduce(amax[:], w3, AX.X, ALU.max, apply_absolute_value=True)
        vv.drain()
        vv.tensor_scalar(base_r[:], amax[:], 1e-8, None, ALU.max)
        vv.drain()
        vv.tensor_scalar(base[:], base_r[:], 1.0 / 15.0, None, ALU.mult)
        vv.drain()
        vv.reciprocal(rbase[:], base[:])
        vv.drain()
        vv.tensor_tensor(u3, w3, bcx(rbase), ALU.mult).then_inc(S["qtv"].h, 1)
        vv.drain()

        def kd(j):
            sl = (j % 2) * 2048
            vv.wait_ge(S["qts"].h, 32 * p + q_tnum(j))
            vv.tensor_scalar(qk[:], qt[:, sl:sl + 2048], QCLIP, -QCLIP,
                             ALU.min, ALU.max).then_inc(S["qtv"].h, 1)
            vv.scalar_tensor_tensor(qd[:, sl:sl + 2048], qk[:], float(SHRINKS[j]),
                                    qu[:], ALU.mult, ALU.subtract).then_inc(S["qtv"].h, 1)
        e3 = fixs.ap().rearrange("p (b j) -> p b j", j=16)
        kd(0)
        for j in range(16):
            if j < 15:
                kd(j + 1)
            sl = (j % 2) * 2048
            vv.wait_ge(S["qts"].h, 32 * p + q_dnum(j))
            vv.tensor_reduce(
                e3[:, :, j:j + 1],
                qd2.ap()[:, sl:sl + 2048].rearrange("p (b k) -> p b k", k=64),
                AX.X, ALU.add)
        vv.drain()
        vv.tensor_reduce(bEa[:], e3, AX.X, ALU.min)
        vv.drain()
        md = qd.ap()[:, 0:512].rearrange("p (b j) -> p b j", j=16)
        vv.tensor_tensor(md, e3, bEa.ap().unsqueeze(2).broadcast_to([128, 32, 16]),
                         ALU.is_equal)
        vv.drain()
        vv.tensor_tensor(qd[:, 512:1024], qd[:, 0:512], qsh[:], ALU.mult)
        vv.tensor_scalar(qd[:, 1024:1536], qd[:, 0:512], -2.0, 2.0,
                         ALU.mult, ALU.add)
        vv.drain()
        vv.tensor_tensor(qd[:, 1536:2048], qd[:, 512:1024], qd[:, 1024:1536],
                         ALU.add)
        vv.drain()
        vv.tensor_reduce(bcba[:],
                         qd.ap()[:, 1536:2048].rearrange("p (b j) -> p b j", j=16),
                         AX.X, ALU.min)
        vv.drain()
        vv.tensor_tensor(tmp[:], base[:], bcba[:], ALU.mult)
        vv.drain()
        vv.reciprocal(tmp2[:], bcba[:])
        vv.drain()
        vv.tensor_tensor(t3, u3, bcx(tmp2), ALU.mult)
        vv.tensor_scalar(qk[:], qt[:, 0:2048], QCLIP, -QCLIP, ALU.min, ALU.max)
        vv.drain()
        vv.tensor_tensor(wq3, k3, bcx(tmp), ALU.mult).then_inc(S["qp"].h, 1)

    TRM0 = S["mm"].n
    TRQ0 = S["trp"].n

    def phAq(vv):
        for p in range(16):
            vv.wait_ge(S["sld"].h, QLD0 + 16 * (p + 1))
            if p >= 1:
                vv.wait_ge(S["mm"].h, TRM0 + 16 * p)
            qpiece(vv, p)
    on("vector", phAq)
    S["qp"].bump(16)
    S["qtv"].bump(33 * 16)

    def phAtr(e):
        e.wait_ge(S["sld"].h, LD_CONSTS)
        for p in range(16):
            e.wait_ge(S["qp"].h, p + 1)
            for i in range(16):
                n = 16 * p + i
                if n >= 1:
                    e.wait_ge(S["trp"].h, TRQ0 + n)
                e.transpose(ps_tr[:, 0:128], qwq[:, i * 128:(i + 1) * 128],
                            ident[:]).then_inc(S["mm"].h, 1)
    on("tensor", phAtr)
    S["mm"].bump(256)

    def phAs(s):
        s.wait_ge(S["sld"].h, LD_CONSTS)
        for p in range(16):
            def t_of(j):
                s.activation(qt[:, (j % 2) * 2048:(j % 2) * 2048 + 2048], qu[:],
                             AF.Copy, scale=qrj[:, j:j + 1]).then_inc(S["qts"].h, 1)
            s.wait_ge(S["qtv"].h, 33 * p + 1)
            t_of(0)
            t_of(1)
            for j in range(14):
                s.wait_ge(S["qtv"].h, 33 * p + 3 + 2 * j)
                sl = (j % 2) * 2048
                s.activation(qd2[:, sl:sl + 2048], qd[:, sl:sl + 2048],
                             AF.Square).then_inc(S["qts"].h, 1)
                t_of(j + 2)
            for j in (14, 15):
                s.wait_ge(S["qtv"].h, 33 * p + 3 + 2 * j)
                sl = (j % 2) * 2048
                s.activation(qd2[:, sl:sl + 2048], qd[:, sl:sl + 2048],
                             AF.Square).then_inc(S["qts"].h, 1)
            for i in range(16):
                n = 16 * p + i
                s.wait_ge(S["mm"].h, TRM0 + n + 1)
                if n >= 2:
                    s.wait_ge(S["qst"].h, 16 * (n - 1))
                s.activation(qtt[:, (i % 2) * 128:(i % 2) * 128 + 128],
                             ps_tr[:, 0:128], AF.Copy).then_inc(S["trp"].h, 1)
    on("scalar", phAs)
    S["qts"].bump(32 * 16)
    S["trp"].bump(256)
    assert S["qp"].n == 16

    def qtile_dst(p, i):
        if p < 8:
            return bass.AP(wq_pub.tensor, i * 131072 + p * 128,
                           [[1024, 128], [1, 128]])
        rt, cc4 = divmod(p - 8, 4)
        ht = cc4 * 16 + i
        return bass.AP(wq_pub.tensor, FCR * C + ht * 32768 + rt * 128,
                       [[256, 128], [1, 128]])

    def phAst(e):
        for p in range(16):
            for i in range(16):
                n = 16 * p + i
                e.wait_ge(S["trp"].h, TRQ0 + n + 1)
                e.dma_start(out=qtile_dst(p, i),
                            in_=qtt[:, (i % 2) * 128:(i % 2) * 128 + 128]
                            ).then_inc(S["qst"].h, 16)
        e.wait_ge(S["qst"].h, 16 * 256)
        e.collective_compute(
            "AllGather", ALU.bypass,
            replica_groups=[list(range(NCOR))],
            ins=[wq_pub], outs=[wq_all],
        ).then_inc(S["cc"].h, 1)
    on("gpsimd", phAst)
    S["qst"].bump(16 * 256)
    S["cc"].bump(1)

    if "wq" in DBG:
        def dbgwq(e):
            e.wait_ge(S["qst"].h, 16 * 256)
            e.dma_start(out=dbg_o["wq"], in_=wq_pub).then_inc(S["st"].h, 16)
        on("sync", dbgwq)
        S["st"].bump(16)

    if PHMAX <= 1:
        return _assemble()

    def phBxt(e):
        for tc in range(2, NTC):
            e.wait_ge(S["evac"].h, tc - 1)
            e.dma_start(out=xtb[:, (tc % 2) * 8192:(tc % 2) * 8192 + 8192],
                        in_=xt_t[tc]).then_inc(S["gld"].h, 16)
    on("gpsimd", phBxt)
    S["gld"].bump(16 * (NTC - 2))

    def GLD_XT(tc):
        return GLD_WCB + 16 * (tc + 1)

    def phBcol(e):
        e.wait_ge(S["gld"].h, GLD_WCB)
        ins = None
        for dt in range(6):
            for ct in range(16):
                ins = e.matmul(ps_l[:, dt:dt + 1],
                               wcb[:, ct * 768 + dt * 128: ct * 768 + (dt + 1) * 128],
                               onescol[:], start=(ct == 0), stop=(ct == 15))
        ins.then_inc(S["mm"].h, 1)
    on("tensor", phBcol)
    S["mm"].bump(1)
    MM_COLS = S["mm"].n

    def phBcolv(vv):
        vv.wait_ge(S["mm"].h, MM_COLS)
        vv.tensor_copy(colsum[:, 0:6], ps_l[:, 0:6]).then_inc(S["vx"].h, 1)
    on("vector", phBcolv)
    S["vx"].bump(1)
    VX0 = S["vx"].n

    def phBsq(sx):
        for tc in range(NTC):
            xv = xtb.ap()[:, (tc % 2) * 8192:(tc % 2) * 8192 + 8192]
            sx.wait_ge(S["gld"].h, GLD_XT(tc))
            for ct in range(16):
                n = tc * 16 + ct
                if n >= 2:
                    sx.wait_ge(S["sq"].h, n - 1)
                sx.activation(sqs[:, (ct % 2) * 512:(ct % 2) * 512 + 512],
                              xv[:, ct * 512:(ct + 1) * 512], AF.Square).then_inc(S["exs"].h, 1)
            sx.wait_ge(S["vx"].h, VX0 + VX_PER_TC * tc + 1)
            sx.activation(rows[0:1, 3072:3584], rows[0:1, 2560:3072], AF.Ln,
                          bias=csts[0:1, CZERO:CZERO + 1])
            sx.drain()
            sx.activation(rows[0:1, 3584:4096], rows[0:1, 3072:3584], AF.Exp,
                          scale=csts[0:1, CNEGH:CNEGH + 1]).then_inc(S["exs"].h, 1)
    on("scalar", phBsq)
    S["exs"].bump(17 * NTC)
    EXS_SQ = S["exs"].n

    def EXS_SQN(tc, ct):
        return 17 * tc + ct + 1

    def EXS_RSTD(tc):
        return 17 * tc + 17

    QKV_PS = [ps_o[0], ps_o[1], ps_o[2], ps_o[3], None, None]

    def qkv_ps_ap(dt):
        if dt < 4:
            return ps_o[dt][:]
        return ps_s[:, (dt - 4) * 512:(dt - 4) * 512 + 512]

    VX_PER_TC = 4

    def phBmm(e):
        for tc in range(NTC):
            xv = xtb.ap()[:, (tc % 2) * 8192:(tc % 2) * 8192 + 8192]
            e.wait_ge(S["gld"].h, GLD_XT(tc))
            if tc == 0:
                e.wait_ge(S["vx"].h, VX0)
            if tc >= 1:
                e.wait_ge(S["evac"].h, tc)
            for ct in range(16):
                rhs = xv[:, ct * 512:(ct + 1) * 512]
                for dt in range(6):
                    e.matmul(qkv_ps_ap(dt),
                             wcb[:, ct * 768 + dt * 128: ct * 768 + (dt + 1) * 128],
                             rhs, start=(ct == 0), stop=(ct == 15))
                e.matmul(ps_l[0:1, :], onescol[:], rhs, start=(ct == 0), stop=(ct == 15))
                e.wait_ge(S["exs"].h, EXS_SQN(tc, ct))
                e.matmul(ps_l[32:33, :], onescol[:],
                         sqs[:, (ct % 2) * 512:(ct % 2) * 512 + 512],
                         start=(ct == 0), stop=(ct == 15)).then_inc(S["sq"].h, 1)
            e.wait_ge(S["vx"].h, VX0 + VX_PER_TC * tc + 2)
            e.matmul(ps_l[:, :], onesrow32[:], rows[0:1, 0:512],
                     start=True, stop=True).then_inc(S["mm"].h, 1)
            e.wait_ge(S["vx"].h, VX0 + VX_PER_TC * tc + 3)
            e.matmul(ps_l[:, :], onesrow32[:], rows[0:1, 512:1024],
                     start=True, stop=True).then_inc(S["mm"].h, 1)
    on("tensor", phBmm)
    S["sq"].bump(16 * NTC)
    S["mm"].bump(2 * NTC)

    def MM_BC(tc, which):
        return MM_COLS + 2 * tc + 1 + which

    dsts = [q1, q2, k1, k2, va, vb]

    def phBvec(vv):
        for tc in range(NTC):
            vv.wait_ge(S["sq"].h, 16 * (tc + 1))
            r = rows.ap()
            vv.tensor_scalar(r[0:1, 0:512], ps_l[0:1, :], 1.0 / 2048.0, None, ALU.mult)
            vv.tensor_scalar(r[0:1, 1024:1536], ps_l[32:33, :], 1.0 / 2048.0, None, ALU.mult)
            vv.drain()
            vv.tensor_tensor(r[0:1, 1536:2048], r[0:1, 0:512], r[0:1, 0:512], ALU.mult)
            vv.drain()
            vv.tensor_tensor(r[0:1, 2048:2560], r[0:1, 1024:1536], r[0:1, 1536:2048], ALU.subtract)
            vv.drain()
            vv.tensor_scalar(r[0:1, 2560:3072], r[0:1, 2048:2560], LN_EPS, None, ALU.add)
            vv.drain()
            vv.tensor_copy(r[0:1, 4 * 512:4 * 512 + 4], r[0:1, 0:4]).then_inc(S["vx"].h, 1)
            vv.wait_ge(S["exs"].h, EXS_RSTD(tc))
            vv.drain()
            vv.tensor_scalar(r[0:1, 512:1024], r[0:1, 3584:4096], -1.0, None, ALU.mult).then_inc(S["vx"].h, 1)
            vv.wait_ge(S["mm"].h, MM_BC(tc, 0))
            vv.tensor_copy(bca[:], ps_l[:, :]).then_inc(S["vx"].h, 1)
            vv.wait_ge(S["mm"].h, MM_BC(tc, 1))
            vv.tensor_copy(bcb[:], ps_l[:, :])
            vv.drain()
            ins = None
            for dt in range(6):
                vv.scalar_tensor_tensor(fixs[:], bca[:], colsum[:, dt:dt + 1],
                                        qkv_ps_ap(dt), ALU.mult, ALU.subtract)
                ins = vv.tensor_tensor(dsts[dt][:, tc * 512:(tc + 1) * 512],
                                       fixs[:], bcb[:], ALU.mult)
            ins.then_inc(S["evac"].h, 1)
            vv.tensor_copy(r[0:1, 4 * 512 + 4:4 * 512 + 8], r[0:1, 0:4]).then_inc(S["vx"].h, 1)
    on("vector", phBvec)
    S["vx"].bump(VX_PER_TC * NTC)
    S["evac"].bump(NTC)
    VX_B = S["vx"].n

    for nm, buf in [("q1b", q1), ("vab", va)]:
        if nm in DBG:
            def dfnb(e, nm=nm, buf=buf):
                e.wait_ge(S["evac"].h, NTC)
                e.dma_start(out=dbg_o[nm], in_=buf[:]).then_inc(S["st"].h, 16)
            on("sync", dfnb)
            S["st"].bump(16)
    for nm, buf in [("cols", colsum), ("bca", bca), ("bcb", bcb)]:
        if nm in DBG:
            def dfnc(e, nm=nm, buf=buf):
                e.wait_ge(S["evac"].h, NTC)
                e.dma_start(out=dbg_o[nm], in_=buf[:]).then_inc(S["st"].h, 16)
            on("sync", dfnc)
            S["st"].bump(16)
    if "wcb" in DBG:
        def dfnw(e):
            e.wait_ge(S["gld"].h, GLD_WCB)
            e.dma_start(out=dbg_o["wcb"], in_=wcb[:]).then_inc(S["st"].h, 16)
        on("sync", dfnw)
        S["st"].bump(16)
    if "rows0" in DBG:
        def dfnr(e):
            e.wait_ge(S["evac"].h, NTC)
            e.dma_start(out=dbg_o["rows0"], in_=rows[0:1, :]).then_inc(S["st"].h, 16)
        on("sync", dfnr)
        S["st"].bump(16)

    if PHMAX <= 2:
        return _assemble()

    def phCld(e):
        e.wait_ge(S["evac"].h, NTC)
        e.dma_start(out=cosb[:], in_=cos_i).then_inc(S["sld"].h, 16)
        e.dma_start(out=sinb[:], in_=sin_i).then_inc(S["sld"].h, 16)
    on("sync", phCld)
    S["sld"].bump(32)
    LD_COS = S["sld"].n

    if PHC_SUB <= 1:
        return _assemble()
    def phClamV(vv):
        vv.wait_ge(S["sld"].h, LD_CONSTS)
        r = rows.ap()
        vv.tensor_tensor(r[0:1, 0:256], lr[0:1, 0:256], lr[0:1, 256:512], ALU.mult)
        vv.drain()
        vv.tensor_reduce(r[0:1, 512:513], r[0:1, 0:256].unsqueeze(1), AX.X, ALU.add)
        vv.drain()
        vv.tensor_tensor(r[0:1, 256:512], lr[0:1, 512:768], lr[0:1, 768:1024], ALU.mult)
        vv.drain()
        vv.tensor_reduce(r[0:1, 513:514], r[0:1, 256:512].unsqueeze(1), AX.X, ALU.add)
        vv.drain()
        vv.tensor_copy(r[0:1, 514:516], r[0:1, 512:514]).then_inc(S["vx"].h, 1)
    on("vector", phClamV)
    S["vx"].bump(1)
    VX_LAM = S["vx"].n

    def phClamS(s):
        s.wait_ge(S["vx"].h, VX_LAM)
        s.activation(rows[0:1, 516:517], rows[0:1, 514:515], AF.Exp)
        s.activation(rows[0:1, 517:518], rows[0:1, 515:516], AF.Exp).then_inc(S["exs"].h, 1)
    on("scalar", phClamS)
    S["exs"].bump(1)
    EXS_LAM = S["exs"].n

    def phClamV2(vv):
        vv.wait_ge(S["exs"].h, EXS_LAM)
        vv.drain()
        r = rows.ap()
        vv.tensor_tensor(r[0:1, 518:519], r[0:1, 516:517], r[0:1, 517:518], ALU.subtract)
        vv.drain()
        vv.tensor_scalar(lamv[0:1, 0:1], r[0:1, 518:519], LAM_INIT, None, ALU.add)
        vv.drain()
        vv.tensor_scalar(lamv[0:1, 1:2], lamv[0:1, 0:1], -1.0, None, ALU.mult)
        vv.drain()
        if "drow" in DBG:
            vv.tensor_copy(lamv[0:1, 4:8], lr[0:1, 0:4])
            vv.tensor_copy(lamv[0:1, 8:15], r[0:1, 512:519])
            vv.drain()
        vv.tensor_copy(lamv[0:1, 2:3], lamv[0:1, 0:1]).then_inc(S["vx"].h, 1)
    on("vector", phClamV2)
    S["vx"].bump(1)
    VX_LAM2 = S["vx"].n

    if PHC_SUB <= 2:
        return _assemble()
    RR0_sq = S["exs"].n
    RR0_mm = S["mm"].n
    RR0_vx = S["vx"].n
    RR0_rs = None
    taus = [q1, q2, k1, k2]

    def phCsq(sx):
        n = 0
        for ti in range(4):
            for tc in range(NTC):
                if n == 0:
                    sx.wait_ge(S["evac"].h, NTC)
                if n >= 2:
                    sx.wait_ge(S["sq"].h, 16 * NTC + n - 1)
                sx.activation(sqs[:, (n % 2) * 512:(n % 2) * 512 + 512],
                              taus[ti][:, tc * 512:(tc + 1) * 512], AF.Square).then_inc(S["exs"].h, 1)
                sx.wait_ge(S["vx"].h, RR0_vx + 2 * n + 1)
                sx.activation(rows[0:1, 3072:3584], rows[0:1, 2560:3072], AF.Ln,
                              bias=csts[0:1, CEPS_RMS:CEPS_RMS + 1])
                sx.drain()
                sx.activation(rows[0:1, 3584:4096], rows[0:1, 3072:3584], AF.Exp,
                              scale=csts[0:1, CNEGH:CNEGH + 1]).then_inc(S["exs"].h, 1)
                n += 1
    on("scalar", phCsq)
    S["exs"].bump(2 * 64)
    RR_SQ_END = S["exs"].n

    def phCmm(e):
        n = 0
        for ti in range(4):
            for tc in range(NTC):
                if n >= 1:
                    e.wait_ge(S["vx"].h, RR0_vx + 2 * (n - 1) + 2)
                e.wait_ge(S["exs"].h, RR0_sq + 2 * n + 1)
                e.matmul(ps_l[0:1, :], onescol[:],
                         sqs[:, (n % 2) * 512:(n % 2) * 512 + 512],
                         start=True, stop=True).then_inc(S["sq"].h, 1)
                e.wait_ge(S["vx"].h, RR0_vx + 2 * n + 1)
                e.wait_ge(S["exs"].h, RR0_sq + 2 * n + 2)
                e.matmul(ps_l[:, :], onesrow32[:], rows[0:1, 3584:4096],
                         start=True, stop=True).then_inc(S["mm"].h, 1)
                n += 1
    on("tensor", phCmm)
    S["sq"].bump(64)
    S["mm"].bump(64)

    def phCrow(vv):
        n = 0
        for ti in range(4):
            for tc in range(NTC):
                vv.wait_ge(S["sq"].h, 16 * NTC + n + 1)
                r = rows.ap()
                vv.tensor_scalar(r[0:1, 2560:3072], ps_l[0:1, :], 1.0 / 128.0, None, ALU.mult)
                vv.drain()
                vv.tensor_copy(r[0:1, 520:524], r[0:1, 0:4]).then_inc(S["vx"].h, 1)
                vv.wait_ge(S["mm"].h, RR0_mm + n + 1)
                if n == 0:
                    vv.wait_ge(S["sld"].h, LD_COS)
                cs = cosb.ap()[0:64, tc * 512:(tc + 1) * 512]
                sn = sinb.ap()[0:64, tc * 512:(tc + 1) * 512]
                x1v = taus[ti].ap()[0:64, tc * 512:(tc + 1) * 512]
                x2v = taus[ti].ap()[64:128, tc * 512:(tc + 1) * 512]
                A = pbuf.ap()[0:64, 2048:2560]
                Bs = pbuf.ap()[0:64, 2560:3072]
                Cs = pbuf.ap()[0:64, 3072:3584]
                X2 = pbuf.ap()[0:64, 3584:4096]
                vv.tensor_copy(X2, x2v)
                vv.tensor_tensor(A, x1v, cs, ALU.mult)
                vv.tensor_tensor(Bs, X2, sn, ALU.mult)
                vv.tensor_tensor(Cs, A, Bs, ALU.add)
                vv.tensor_tensor(A, X2, cs, ALU.mult)
                vv.tensor_tensor(Bs, x1v, sn, ALU.mult)
                vv.tensor_tensor(X2, A, Bs, ALU.subtract)
                vv.drain()
                vv.tensor_tensor(x1v, Cs, ps_l[0:64, :], ALU.mult)
                vv.tensor_tensor(A, X2, ps_l[0:64, :], ALU.mult)
                vv.drain()
                vv.tensor_copy(x2v, A).then_inc(S["vx"].h, 1)
                n += 1
    on("vector", phCrow)
    S["vx"].bump(2 * 64)
    VX_RR_END = S["vx"].n

    if PHC_SUB <= 3:
        return _assemble()
    TRP0 = S["mm"].n
    VTC0 = S["vx"].n

    def phCvt(e):
        for kt in range(64):
            if kt >= 1:
                e.wait_ge(S["vx"].h, VTC0 + kt)
            e.transpose(ps_tr[:, 0:128], va[:, kt * 128:(kt + 1) * 128], ident[:])
            e.transpose(ps_tr[:, 128:256], vb[:, kt * 128:(kt + 1) * 128],
                        ident[:]).then_inc(S["mm"].h, 1)
    on("tensor", phCvt)
    S["mm"].bump(64)

    def phCvtv(vv):
        for kt in range(64):
            vv.wait_ge(S["mm"].h, TRP0 + kt + 1)
            vv.tensor_copy(vt[:, kt * 256:kt * 256 + 256],
                           ps_tr[:, 0:256]).then_inc(S["vx"].h, 1)
    on("vector", phCvtv)
    S["vx"].bump(64)
    VX_VT_END = S["vx"].n

    for nm, buf in [("q1", q1), ("q2", q2), ("k1", k1), ("k2", k2), ("va", va), ("vb", vb)]:
        if nm in DBG:
            def dfn(e, nm=nm, buf=buf):
                e.wait_ge(S["vx"].h, VX_VT_END)
                e.dma_start(out=dbg_o[nm], in_=buf[:]).then_inc(S["st"].h, 16)
            on("sync", dfn)
            S["st"].bump(16)

    if PHMAX <= 3:
        return _assemble()

    NKT = [4 * (c + 1) for c in range(NTC)]
    GOF = [0]
    for c in range(NTC):
        GOF.append(GOF[-1] + NKT[c])
    NSTEP = GOF[-1]

    EXP0 = S["expn"].n
    PV0 = S["pv"].n
    MM_D0 = S["mm"].n
    VMASK0 = S["vmask"].n
    VX_D0 = S["vx"].n

    def is_diag(c, kt):
        return kt >= NKT[c] - 4

    dmask_count = {}
    nmask = 0
    for c in range(NTC):
        for kt in range(NKT[c]):
            if is_diag(c, kt):
                nmask += 1
                dmask_count[(c, kt)] = nmask

    def pv_group(e, c, kt, first, last):
        g = GOF[c] + kt
        if is_diag(c, kt):
            e.wait_ge(S["vmask"].h, VMASK0 + dmask_count[(c, kt)])
            slot = 2 + (g % 2)
        else:
            slot = g % 2
        pv = pbuf.ap()[:, slot * 1024:(slot + 1) * 1024]
        st_, sp = first, last
        e.matmul(ps_o[0][:], vt[:, kt * 256:kt * 256 + 128], pv[:, 0:512], start=st_, stop=sp)
        e.matmul(ps_o[1][:], vt[:, kt * 256 + 128:kt * 256 + 256], pv[:, 0:512], start=st_, stop=sp)
        e.matmul(ps_o[2][:], vt[:, kt * 256:kt * 256 + 128], pv[:, 512:1024], start=st_, stop=sp)
        e.matmul(ps_o[3][:], vt[:, kt * 256 + 128:kt * 256 + 256], pv[:, 512:1024], start=st_, stop=sp)
        e.matmul(ps_l[0:1, :], onescol[:], pv[:, 0:512], start=st_, stop=sp)
        ins = e.matmul(ps_l[32:33, :], onescol[:], pv[:, 512:1024], start=st_, stop=sp)
        return ins

    def phDt(e):
        e.wait_ge(S["vx"].h, VX_VT_END)
        for c in range(NTC):
            e.wait_ge(S["vy"].h, c)
            qs1 = q1.ap()[:, c * 512:(c + 1) * 512]
            qs2 = q2.ap()[:, c * 512:(c + 1) * 512]
            for kt in range(NKT[c]):
                g = GOF[c] + kt
                e.wait_ge(S["expn"].h, EXP0 + g)
                e.matmul(ps_s[:, 0:512], k1[:, kt * 128:(kt + 1) * 128], qs1,
                         start=True, stop=True)
                e.matmul(ps_s[:, 512:1024], k2[:, kt * 128:(kt + 1) * 128], qs2,
                         start=True, stop=True).then_inc(S["mm"].h, 1)
                if kt >= 1:
                    ins = pv_group(e, c, kt - 1, kt - 1 == 0, kt - 1 == NKT[c] - 1)
                    if kt - 1 == NKT[c] - 1:
                        ins.then_inc(S["pv"].h, 1)
            e.wait_ge(S["expn"].h, EXP0 + GOF[c + 1])
            ins = pv_group(e, c, NKT[c] - 1, NKT[c] - 1 == 0, True)
            ins.then_inc(S["pv"].h, 1)
            e.wait_ge(S["vx"].h, VX_D0 + 2 * c + 1)
            e.matmul(ps_s[:, 0:512], onesrow32[:], rows[0:1, 1024:1536],
                     start=True, stop=True)
            e.matmul(ps_s[:, 512:1024], onesrow32[:], rows[0:1, 1536:2048],
                     start=True, stop=True).then_inc(S["mm"].h, 1)
    on("tensor", phDt)
    for c in range(NTC):
        S["mm"].bump(NKT[c])
        S["pv"].bump(1)
        S["mm"].bump(1)

    def MM_S2(c, kt):
        base = MM_D0
        for cc2 in range(c):
            base += NKT[cc2] + 1
        return base + kt + 1

    def MM_DBC(c):
        base = MM_D0
        for cc2 in range(c):
            base += NKT[cc2] + 1
        return base + NKT[c] + 1

    def phDs(s):
        for c in range(NTC):
            for kt in range(NKT[c]):
                g = GOF[c] + kt
                s.wait_ge(S["mm"].h, MM_S2(c, kt))
                s.activation(pbuf[:, (g % 2) * 1024:(g % 2) * 1024 + 1024],
                             ps_s[:, :], AF.Exp,
                             scale=csts[:, CINV_SQ:CINV_SQ + 1]).then_inc(S["expn"].h, 1)
    on("scalar", phDs)
    S["expn"].bump(NSTEP)

    def phDcomb(vv):
        for c in range(NTC):
            for kt in range(NKT[c]):
                if not is_diag(c, kt):
                    continue
                g = GOF[c] + kt
                d = kt - 4 * c
                vv.wait_ge(S["expn"].h, EXP0 + g + 1)
                vv.tensor_tensor(pbuf[:, (2 + g % 2) * 1024:(2 + g % 2) * 1024 + 1024],
                                 pbuf[:, (g % 2) * 1024:(g % 2) * 1024 + 1024],
                                 masks[:, d * 1024:(d + 1) * 1024],
                                 ALU.mult).then_inc(S["vmask"].h, 1)
            vv.wait_ge(S["pv"].h, PV0 + c + 1)
            r = rows.ap()
            vv.tensor_copy(r[0:1, 0:512], ps_l[0:1, :])
            vv.tensor_copy(r[0:1, 512:1024], ps_l[32:33, :])
            vv.drain()
            vv.reciprocal(r[0:1, 1024:1536], r[0:1, 0:512])
            vv.drain()
            vv.reciprocal(r[0:1, 2048:2560], r[0:1, 512:1024])
            vv.drain()
            vv.tensor_scalar(r[0:1, 1536:2048], r[0:1, 2048:2560],
                             lamv[0:1, 1:2], None, ALU.mult)
            vv.drain()
            if c == 10 and "drow" in DBG:
                vv.tensor_copy(r[0:1, 2560:3072], r[0:1, 512:1024])
                vv.tensor_copy(r[0:1, 3072:3584], r[0:1, 1536:2048])
                vv.tensor_copy(r[0:1, 3584:3600], lamv[0:1, 0:16])
                vv.tensor_copy(r[0:1, 3600:3612], r[0:1, 0:12])
                vv.drain()
            vv.tensor_copy(r[0:1, 528:532], r[0:1, 0:4]).then_inc(S["vx"].h, 1)
            vv.wait_ge(S["mm"].h, MM_DBC(c))
            vv.tensor_copy(bca[:], ps_s[:, 0:512])
            vv.tensor_copy(bcb[:], ps_s[:, 512:1024])
            vv.drain()
            pbt = pbuf.ap().bitcast(F32)[:, 1024:1536]
            ins = None
            for half in range(2):
                vv.tensor_tensor(fixs[:], ps_o[0 + half][:], bca[:], ALU.mult)
                vv.tensor_tensor(pbt, ps_o[2 + half][:], bcb[:], ALU.mult)
                ydst = (y1 if half == 0 else y2).ap()[:, c * 512:(c + 1) * 512]
                ins = vv.tensor_tensor(ydst, fixs[:], pbt, ALU.add)
            ins.then_inc(S["vy"].h, 1)
            vv.tensor_copy(r[0:1, 532:536], r[0:1, 0:4]).then_inc(S["vx"].h, 1)
    on("vector", phDcomb)
    S["vmask"].bump(nmask)
    S["vx"].bump(2 * NTC)
    S["vy"].bump(NTC)

    def phDyst(e):
        for c in range(NTC):
            e.wait_ge(S["vy"].h, c + 1)
            e.dma_start(out=y_pub[0:128, c * 512:(c + 1) * 512],
                        in_=y1[:, c * 512:(c + 1) * 512]).then_inc(S["yst"].h, 16)
            e.dma_start(out=y_pub[128:256, c * 512:(c + 1) * 512],
                        in_=y2[:, c * 512:(c + 1) * 512]).then_inc(S["yst"].h, 16)
    on("sync", phDyst)
    S["yst"].bump(32 * NTC)

    for nm, buf in [("y1", y1), ("y2", y2)]:
        if nm in DBG:
            def dfn(e, nm=nm, buf=buf):
                e.wait_ge(S["vy"].h, NTC)
                e.dma_start(out=dbg_o[nm], in_=buf[:]).then_inc(S["st"].h, 16)
            on("sync", dfn)
            S["st"].bump(16)
    if "drow" in DBG:
        def dfnd(e):
            e.wait_ge(S["vy"].h, NTC)
            e.dma_start(out=dbg_o["drow"], in_=rows[0:1, :]).then_inc(S["st"].h, 16)
        on("sync", dfnd)
        S["st"].bump(16)

    def phDag(e):
        e.wait_ge(S["yst"].h, 32 * NTC)
        e.collective_compute(
            "AllGather", ALU.bypass,
            replica_groups=[list(range(NCOR))],
            ins=[y_pub], outs=[y_all],
        ).then_inc(S["cc2"].h, 1)
    on("gpsimd", phDag)
    S["cc2"].bump(1)

    if PHMAX <= 4:
        return _assemble()

    WOD0 = S["yst"].n
    YT0 = S["qst"].n

    def phEyt(e):
        e.wait_ge(S["cc2"].h, 1)
        for arm in e.switch_core_id(n=128):
            rr = arm.logical % NCOR
            for g in range(16):
                src = bass.AP(y_all.tensor, (g * 128) * T + rr * TM, [[T, 128], [1, TM]])
                e.dma_start(out=yT[:, g * 1024:(g + 1) * 1024], in_=src).then_inc(S["qst"].h, 16)
    on("sync", phEyt)
    S["qst"].bump(16 * 16)
    YT_END = S["qst"].n

    GLD_WO0 = S["gld"].n

    def phEwo_ld(e):
        e.wait_ge(S["vy"].h, NTC)
        for ct2 in range(16):
            if ct2 >= 2:
                e.wait_ge(S["fcx"].h, ct2 - 1)
            e.dma_start(out=woTb[:, (ct2 % 2) * 2048:(ct2 % 2) * 2048 + 2048],
                        in_=woT_t[ct2]).then_inc(S["gld"].h, 16)
    on("gpsimd", phEwo_ld)
    S["gld"].bump(16 * 16)

    def GLD_WO(ct2):
        return GLD_WO0 + 16 * (ct2 + 1)

    EXS_WO0 = S["exs"].n

    def phEwo_mm(e):
        e.wait_ge(S["qst"].h, YT_END)
        for ct2 in range(16):
            e.wait_ge(S["gld"].h, GLD_WO(ct2))
            if ct2 >= 1:
                e.wait_ge(S["exs"].h, EXS_WO0 + ct2)
            wv = woTb.ap()[:, (ct2 % 2) * 2048:(ct2 % 2) * 2048 + 2048]
            ins = None
            for ci in range(16):
                for hf in range(2):
                    ins = e.matmul(ps_s[:, hf * 512:(hf + 1) * 512],
                                   wv[:, ci * 128:(ci + 1) * 128],
                                   yT[:, ci * 1024 + hf * 512: ci * 1024 + (hf + 1) * 512],
                                   start=(ci == 0), stop=(ci == 15))
            ins.then_inc(S["fcx"].h, 1)
    on("tensor", phEwo_mm)
    S["fcx"].bump(16)
    FCX_WO = S["fcx"].n

    def phEwo_ev(s):
        for ct2 in range(16):
            s.wait_ge(S["fcx"].h, ct2 + 1)
            s.activation(attnT[:, ct2 * 1024:(ct2 + 1) * 1024], ps_s[:, :],
                         AF.Copy, scale=csts[:, C08:C08 + 1]).then_inc(S["exs"].h, 1)
    on("scalar", phEwo_ev)
    S["exs"].bump(16)
    EXS_WO_END = S["exs"].n

    if "attnT" in DBG:
        def dfn(e):
            e.wait_ge(S["exs"].h, EXS_WO_END)
            e.dma_start(out=dbg_o["attnT"], in_=attnT[:]).then_inc(S["st"].h, 16)
        on("sync", dfn)
        S["st"].bump(16)

    TRE0 = S["mm"].n
    VTE0 = S["vx"].n

    def phEtr(e):
        e.wait_ge(S["exs"].h, EXS_WO_END)
        e.wait_ge(S["fcx"].h, FCX_WO)
        n = 0
        for tt in range(8):
            for ct2 in range(16):
                if n >= 1:
                    e.wait_ge(S["vx"].h, VTE0 + n)
                e.transpose(ps_tr[:, 0:128],
                            attnT[:, ct2 * 1024 + tt * 128: ct2 * 1024 + (tt + 1) * 128],
                            ident[:]).then_inc(S["mm"].h, 1)
                n += 1
    on("tensor", phEtr)
    S["mm"].bump(128)

    def phEtrv(vv):
        n = 0
        for tt in range(8):
            for ct2 in range(16):
                vv.wait_ge(S["mm"].h, TRE0 + n + 1)
                vv.tensor_copy(attok[:, tt * 2048 + ct2 * 128: tt * 2048 + (ct2 + 1) * 128],
                               ps_tr[:, 0:128]).then_inc(S["vx"].h, 1)
                n += 1
    on("vector", phEtrv)
    S["vx"].bump(128)
    VX_TRE_END = S["vx"].n

    def ln_tokmajor(tag, src_ap, src_sq_f32, dst_ap, cols, eps_col, sgate_vx=None):
        v0 = S["vx"].n
        s0 = S["exs"].n

        def vpart1(vv):
            vv.tensor_reduce(cols[:, 0:1], src_ap, AX.X, ALU.add)
            vv.drain()
            vv.tensor_scalar(cols[:, 2:3], cols[:, 0:1], 1.0 / 2048.0, None, ALU.mult)
            vv.drain()
            vv.tensor_copy(cols[:, 8:9], cols[:, 2:3]).then_inc(S["vx"].h, 1)
        on("vector", vpart1)
        S["vx"].bump(1)

        def spart1(s, sgate_vx=sgate_vx):
            if sgate_vx is not None:
                sem, val = sgate_vx if isinstance(sgate_vx, tuple) else (S["vx"], sgate_vx)
                s.wait_ge(sem.h, val)
            s.activation(src_sq_f32, src_ap, AF.Square).then_inc(S["exs"].h, 1)
        on("scalar", spart1)
        S["exs"].bump(1)

        def vpart2(vv):
            vv.wait_ge(S["exs"].h, s0 + 1)
            vv.tensor_reduce(cols[:, 1:2], src_sq_f32, AX.X, ALU.add)
            vv.drain()
            vv.tensor_scalar(cols[:, 3:4], cols[:, 1:2], 1.0 / 2048.0, None, ALU.mult)
            vv.drain()
            vv.tensor_tensor(cols[:, 4:5], cols[:, 2:3], cols[:, 2:3], ALU.mult)
            vv.drain()
            vv.tensor_tensor(cols[:, 5:6], cols[:, 3:4], cols[:, 4:5], ALU.subtract)
            vv.drain()
            vv.tensor_copy(cols[:, 9:10], cols[:, 5:6]).then_inc(S["vx"].h, 1)
        on("vector", vpart2)
        S["vx"].bump(1)

        def spart2(s):
            s.wait_ge(S["vx"].h, v0 + 2)
            s.activation(cols[:, 6:7], cols[:, 5:6], AF.Ln, bias=eps_col)
            s.drain()
            s.activation(cols[:, 7:8], cols[:, 6:7], AF.Exp,
                         scale=csts[:, CNEGH:CNEGH + 1]).then_inc(S["exs"].h, 1)
        on("scalar", spart2)
        S["exs"].bump(1)

        def vpart3(vv):
            vv.wait_ge(S["exs"].h, s0 + 2)
            vv.drain()
            vv.tensor_tensor(cols[:, 10:11], cols[:, 2:3], cols[:, 7:8], ALU.mult)
            vv.drain()
            vv.tensor_scalar(cols[:, 11:12], cols[:, 10:11], -1.0, None, ALU.mult)
            vv.drain()
            vv.tensor_copy(cols[:, 12:13], cols[:, 11:12]).then_inc(S["vx"].h, 1)
        on("vector", vpart3)
        S["vx"].bump(1)

        def vpart4(vv):
            vv.tensor_scalar(dst_ap, src_ap, cols[:, 7:8], cols[:, 11:12],
                             ALU.mult, ALU.add).then_inc(S["vx"].h, 1)
        on("vector", vpart4)
        S["vx"].bump(1)
        return S["vx"].n, S["exs"].n

    XM0 = S["sld"].n
    TRX0 = S["mm"].n

    def phEln(tt):
        def ld(e, tt=tt):
            if tt >= 1:
                e.wait_ge(S["vx"].h, 0)
            e.wait_ge(S["vy"].h, 0)
            e.dma_start(out=exm[:], in_=x_my.rearrange("(a p) c -> a p c", p=128)[tt]).then_inc(S["sld"].h, 16)
        v_res_prev = S["vx"].n
        def ld2(e, tt=tt, v_res_prev=v_res_prev):
            e.wait_ge(S["vx"].h, VX_TRE_END if tt == 0 else v_res_prev)
            e.dma_start(out=exm[:], in_=x_my.rearrange("(a p) c -> a p c", p=128)[tt]).then_inc(S["sld"].h, 16)
        on("sync", ld2)
        S["sld"].bump(16)
        ld_n = S["sld"].n

        src = attok.ap()[:, tt * 2048:(tt + 1) * 2048]
        def vgate(vv, tt=tt):
            if tt == 0:
                vv.wait_ge(S["vx"].h, VX_TRE_END)
        on("vector", vgate)
        ln_tokmajor("ln2", src, esq[:], eln[:], ecol, csts[:, CEPS_LN:CEPS_LN + 1],
                    sgate_vx=VTE0 + (tt + 1) * 16)
        v_apply2 = S["vx"].n

        def vres(vv, ld_n=ld_n):
            vv.wait_ge(S["sld"].h, ld_n)
            vv.drain()
            vv.tensor_tensor(ex1[:], exm[:], eln[:], ALU.add).then_inc(S["vx"].h, 1)
        on("vector", vres)
        S["vx"].bump(1)
        v_res = S["vx"].n

        def st1(e, tt=tt, v_res=v_res):
            e.wait_ge(S["vx"].h, v_res)
            e.dma_start(out=x1_scr.rearrange("(a p) c -> a p c", p=128)[tt],
                        in_=ex1[:]).then_inc(S["qst"].h, 16)
        on("sync", st1)
        S["qst"].bump(16)

        if "x1" in DBG:
            def dx1(e, tt=tt, v_res=v_res):
                e.wait_ge(S["vx"].h, v_res)
                e.dma_start(out=dbg_o["x1"].rearrange("(a p) c -> a p c", p=128)[tt],
                            in_=ex1[:]).then_inc(S["st"].h, 16)
            on("sync", dx1)
            S["st"].bump(16)

        def vgate3(vv, v_res=v_res):
            vv.wait_ge(S["vx"].h, v_res)
        on("vector", vgate3)

        def sgate3(s, v_res=v_res):
            s.wait_ge(S["vx"].h, v_res)
        on("scalar", sgate3)
        ln_tokmajor("ln3", ex1.ap()[:, :], esq[:], exn2[:],
                    ecol, csts[:, CEPS_LN:CEPS_LN + 1], sgate_vx=None)
        s_apply3 = S["exs"].n

        n0m = S["mm"].n
        n0v = S["vx"].n

        def etr(e, tt=tt, s_apply3=s_apply3, n0v=n0v):
            e.wait_ge(S["exs"].h, s_apply3)
            e.wait_ge(S["vx"].h, n0v)
            for ci in range(16):
                if ci >= 1:
                    e.wait_ge(S["vx"].h, n0v + ci)
                e.transpose(ps_tr[:, 0:128],
                            exn2[:, ci * 128:(ci + 1) * 128], ident[:]).then_inc(S["mm"].h, 1)
        on("tensor", etr)
        S["mm"].bump(16)

        def etrv(vv, tt=tt, n0m=n0m):
            for ci in range(16):
                vv.wait_ge(S["mm"].h, n0m + ci + 1)
                vv.tensor_copy(xn2T[:, ci * 1024 + tt * 128: ci * 1024 + (tt + 1) * 128],
                               ps_tr[:, 0:128]).then_inc(S["vx"].h, 1)
        on("vector", etrv)
        S["vx"].bump(16)

    for tt in range(8):
        phEln(tt)
    VX_E_END = S["vx"].n
    X1ST_END = S["qst"].n

    if "xn2T" in DBG:
        def dxn(e):
            e.wait_ge(S["vx"].h, VX_E_END)
            e.dma_start(out=dbg_o["xn2T"], in_=xn2T[:]).then_inc(S["st"].h, 16)
        on("sync", dxn)
        S["st"].bump(16)

    if PHMAX <= 5:
        return _assemble()

    def wfcq_T(hg, ci):
        rk, h2 = divmod(hg, 2)
        off = rk * WQ_ELS + ci * 131072 + h2 * 512
        return bass.AP(wq_all.tensor, off, [[1024, 128], [1, 512]])

    def wpjq_T(cg, rr2, ht):
        rk = 2 * cg + rr2
        off = rk * WQ_ELS + FCR * C + ht * 32768
        return bass.AP(wq_all.tensor, off, [[256, 128], [1, 256]])

    FCX_WO = S["fcx"].n

    for hh in range(2):
        FWF0 = S["qst"].n
        FCX0 = S["fcx"].n

        def phFfcld(e, hh=hh, FCX0=FCX0):
            if hh == 0:
                e.wait_ge(S["cc"].h, 1)
            for hg in range(16):
                if hg >= 2:
                    e.wait_ge(S["fcx"].h, FCX0 + 4 * (hg - 1))
                for ci in range(16):
                    e.dma_start(
                        out=fwfc[:, (hg % 2) * 8192 + ci * 512:(hg % 2) * 8192 + (ci + 1) * 512],
                        in_=wfcq_T(hg, ci)).then_inc(S["qst"].h, 16)
        on("sync", phFfcld)
        S["qst"].bump(16 * 16 * 16)

        def FWF(hg, FWF0=FWF0):
            return FWF0 + 16 * 16 * (hg + 1)

        VH0 = S["vx"].n
        EXH0 = S["exs"].n

        def phFfc(e, hh=hh, FWF=FWF, VH0=VH0, FCX0=FCX0):
            if hh == 0:
                e.wait_ge(S["vx"].h, VX_E_END)
            for hg in range(16):
                e.wait_ge(S["qst"].h, FWF(hg))
                for h4 in range(4):
                    ht = hg * 4 + h4
                    if ht >= 2:
                        e.wait_ge(S["vx"].h, VH0 + ht - 1)
                    wv = fwfc.ap()[:, (hg % 2) * 8192:(hg % 2) * 8192 + 8192]
                    ins = None
                    for ci in range(16):
                        ins = e.matmul(ps_o[ht % 2][:],
                                       wv[:, ci * 512 + h4 * 128: ci * 512 + (h4 + 1) * 128],
                                       xn2T[:, ci * 1024 + hh * 512: ci * 1024 + (hh + 1) * 512],
                                       start=(ci == 0), stop=(ci == 15))
                    ins.then_inc(S["fcx"].h, 1)
        on("tensor", phFfc)
        S["fcx"].bump(64)

        def phFrelu(vv, hh=hh, VH0=VH0, EXH0=EXH0, FCX0=FCX0):
            for ht in range(64):
                vv.wait_ge(S["fcx"].h, FCX0 + ht + 1)
                if ht >= 1:
                    vv.wait_ge(S["exs"].h, EXH0 + ht)
                vv.tensor_scalar(fhr[:], ps_o[ht % 2][:], 0.0, None, ALU.max).then_inc(S["vx"].h, 1)
        on("vector", phFrelu)
        S["vx"].bump(64)

        def phFsqr(s, hh=hh, VH0=VH0):
            for ht in range(64):
                s.wait_ge(S["vx"].h, VH0 + ht + 1)
                s.activation(fh2[:, ht * 512:(ht + 1) * 512], fhr[:], AF.Square).then_inc(S["exs"].h, 1)
        on("scalar", phFsqr)
        S["exs"].bump(64)
        EXH_END = S["exs"].n

        if "h2" in DBG and hh == 0:
            def dh2(e, EXH_END=EXH_END):
                e.wait_ge(S["exs"].h, EXH_END)
                e.dma_start(out=dbg_o["h2"], in_=fh2[:]).then_inc(S["st"].h, 16)
            on("sync", dh2)
            S["st"].bump(16)

        if PHF_SUB <= 1:
            continue
        def pj_ps(i):
            if i < 2:
                return ps_o[2 + i][:]
            return ps_s[:, (i - 2) * 512:(i - 2) * 512 + 512]

        PJLD0 = S["qst"].n
        PRJ0 = S["prj"].n
        EXPJ0 = S["exs"].n

        def phFpjld(e, hh=hh, PJLD0=PJLD0, PRJ0=PRJ0):
            for cg in range(4):
                for ht in range(64):
                    n = cg * 64 + ht
                    if n >= 2:
                        e.wait_ge(S["prj"].h, PRJ0 + n - 1)
                    for rr2 in range(2):
                        e.dma_start(
                            out=fwpj[:, (n % 2) * 2048 + rr2 * 256:(n % 2) * 2048 + rr2 * 256 + 256],
                            in_=wpjq_T(cg, rr2, ht)).then_inc(S["qst"].h, 16)
        on("sync", phFpjld)
        S["qst"].bump(32 * 256)

        def PJLD(n, PJLD0=PJLD0):
            return PJLD0 + 32 * (n + 1)

        def phFpj(e, hh=hh, PJLD=PJLD, EXH_END=EXH_END, EXPJ0=EXPJ0, PRJ0=PRJ0):
            e.wait_ge(S["exs"].h, EXH_END)
            for cg in range(4):
                if cg >= 1:
                    e.wait_ge(S["exs"].h, EXPJ0 + 4 * cg)
                ins = None
                for ht in range(64):
                    n = cg * 64 + ht
                    e.wait_ge(S["qst"].h, PJLD(n))
                    wv = fwpj.ap()[:, (n % 2) * 2048:(n % 2) * 2048 + 512]
                    for c4 in range(4):
                        ins = e.matmul(pj_ps(c4), wv[:, c4 * 128:(c4 + 1) * 128],
                                       fh2[:, ht * 512:(ht + 1) * 512],
                                       start=(ht == 0), stop=(ht == 63))
                    ins.then_inc(S["prj"].h, 1)
        on("tensor", phFpj)
        S["prj"].bump(256)

        def phFpjev(s, hh=hh, PRJ0=PRJ0):
            for cg in range(4):
                s.wait_ge(S["prj"].h, PRJ0 + (cg + 1) * 64)
                for c4 in range(4):
                    s.activation(fout[:, (cg * 4 + c4) * 512:(cg * 4 + c4 + 1) * 512],
                                 pj_ps(c4), AF.Copy).then_inc(S["exs"].h, 1)
        on("scalar", phFpjev)
        S["exs"].bump(16)
        EXPJ_END = S["exs"].n

        if "outT" in DBG and hh == 0:
            def dot(e, EXPJ_END=EXPJ_END):
                e.wait_ge(S["exs"].h, EXPJ_END)
                e.dma_start(out=dbg_o["outT"], in_=fout[:]).then_inc(S["st"].h, 16)
            on("sync", dot)
            S["st"].bump(16)

        if PHF_SUB <= 2:
            continue
        s_cast = EXPJ_END

        n0m = S["mm"].n
        VT0 = S["trp"].n

        def ftr(e, hh=hh, s_cast=s_cast, VT0=VT0):
            e.wait_ge(S["exs"].h, s_cast)
            n = 0
            for t4 in range(4):
                for ci in range(16):
                    if n >= 1:
                        e.wait_ge(S["trp"].h, VT0 + n)
                    e.transpose(ps_tr[:, 0:128],
                                fout[:, ci * 512 + t4 * 128: ci * 512 + (t4 + 1) * 128],
                                ident[:]).then_inc(S["mm"].h, 1)
                    n += 1
        on("tensor", ftr)
        S["mm"].bump(64)

        for t4 in range(4):
            n0 = n0m + t4 * 16

            def ftrv(vv, t4=t4, n0=n0):
                for ci in range(16):
                    vv.wait_ge(S["mm"].h, n0 + ci + 1)
                    vv.tensor_copy(ftok[:, ci * 128:(ci + 1) * 128],
                                   ps_tr[:, 0:128]).then_inc(S["trp"].h, 1)
            on("vector", ftrv)
            S["trp"].bump(16)
            vv_n = S["trp"].n

            gtt = hh * 4 + t4

            def fx1ld(e, gtt=gtt, vv_n=vv_n):
                e.wait_ge(S["trp"].h, vv_n)
                e.dma_start(out=fx1[:], in_=x1_scr.rearrange("(a p) c -> a p c", p=128)[gtt]).then_inc(S["sld"].h, 16)
            on("gpsimd", fx1ld)
            S["sld"].bump(16)
            ldn = S["sld"].n

            ln_tokmajor("ln4", ftok.ap()[:, :], fsq[:], fot[:], fcol,
                        csts[:, CEPS_LN:CEPS_LN + 1], sgate_vx=(S["trp"], vv_n))

            def fres3(vv, ldn=ldn):
                vv.wait_ge(S["sld"].h, ldn)
                vv.drain()
                vv.tensor_tensor(eln[:], fx1[:], fot[:], ALU.add).then_inc(S["vx"].h, 1)
            on("vector", fres3)
            S["vx"].bump(1)
            vres_n = S["vx"].n

            def fst(e, gtt=gtt, vres_n=vres_n):
                e.wait_ge(S["vx"].h, vres_n)
                e.dma_start(out=out_t.rearrange("(a p) c -> a p c", p=128)[gtt],
                            in_=eln[:]).then_inc(S["st"].h, 16)
            on("gpsimd", fst)
            S["st"].bump(16)
            stn = S["st"].n

            def egate(vv, stn=stn):
                vv.wait_ge(S["st"].h, stn)
            on("vector", egate)

    ST_END = S["st"].n

    def fin(e):
        e.wait_ge(S["st"].h, ST_END)
    on("sync", fin)

    return _assemble()


def _host_inputs(inputs):
    x = np.ascontiguousarray(np.asarray(inputs["L0_attn_x"], dtype=np.float32).reshape(T, C))
    wq = np.asarray(inputs["L0_attn_wq"], dtype=np.float32)
    wk = np.asarray(inputs["L0_attn_wk"], dtype=np.float32)
    wv = np.asarray(inputs["L0_attn_wv"], dtype=np.float32)
    wo = np.asarray(inputs["L0_attn_wo"], dtype=np.float32)
    wfc = np.asarray(inputs["L0_ffn_w_fc"], dtype=np.float32)
    wpj = np.asarray(inputs["L0_ffn_w_proj"], dtype=np.float32)

    xT = np.ascontiguousarray(x.T)
    xt_t = np.ascontiguousarray(
        xT.reshape(16, 128, NTC, TCH).transpose(2, 1, 0, 3)).reshape(NTC, 128, 16 * TCH)
    woT = np.ascontiguousarray(wo.T)
    woT_t = np.ascontiguousarray(
        woT.reshape(16, 128, 16, 128).transpose(2, 1, 0, 3)).reshape(16, 128, 2048)

    lrows = np.concatenate([
        np.asarray(inputs["L0_attn_lq1"], dtype=np.float32).ravel(),
        np.asarray(inputs["L0_attn_lk1"], dtype=np.float32).ravel(),
        np.asarray(inputs["L0_attn_lq2"], dtype=np.float32).ravel(),
        np.asarray(inputs["L0_attn_lk2"], dtype=np.float32).ravel()])[None, :]

    ident = np.eye(128, dtype=BF16NP)
    ones_c = np.ones((128, 1), dtype=BF16NP)
    ones_r = np.ones((1, 128), dtype=BF16NP)
    csts = np.zeros((128, 8), dtype=np.float32)
    csts[:, CEPS_LN] = LN_EPS
    csts[:, CEPS_RMS] = RMS_EPS
    csts[:, CINV_SQ] = 1.0 / np.sqrt(128.0)
    csts[:, C08] = 1.0 - LAM_INIT
    csts[:, CINV_C] = 1.0 / C
    csts[:, CINV_HH] = 1.0 / 128.0
    csts[:, CNEGH] = -0.5
    csts[:, CZERO] = 0.0

    masks = np.zeros((128, 4, 2, 512), dtype=BF16NP)
    qq = np.arange(512)[None, :]
    tk = np.arange(128)[:, None]
    for d in range(4):
        m = (tk + d * 128 <= qq).astype(np.float32)
        masks[:, d, 0, :] = m
        masks[:, d, 1, :] = m
    masks = np.ascontiguousarray(masks.reshape(128, 4096))

    invf = (1.0 / 10000.0) ** (np.arange(0, 128, 2, dtype=np.float32) / 128.0)

    in_maps = []
    for r in range(NCOR):
        wcomb_r = np.concatenate([
            np.ascontiguousarray(wq[r * 256:(r + 1) * 256, :].T),
            np.ascontiguousarray(wk[r * 256:(r + 1) * 256, :].T),
            np.ascontiguousarray(wv[r * 256:(r + 1) * 256, :].T)], axis=1)
        wcomb_t = np.ascontiguousarray(
            wcomb_r.reshape(16, 128, 768)).transpose(1, 0, 2).reshape(128, 16 * 768)
        wcomb_t = np.ascontiguousarray(wcomb_t)
        ang = (invf * np.float32(r)).astype(np.float32)
        cos_t = np.ascontiguousarray(np.tile(np.cos(ang).astype(BF16NP)[:, None], (1, T)))
        sin_t = np.ascontiguousarray(np.tile(np.sin(ang).astype(BF16NP)[:, None], (1, T)))
        in_maps.append(dict(
            x_my=np.ascontiguousarray(x[r * TM:(r + 1) * TM, :]),
            xt_t=xt_t, wcomb=wcomb_t, woT_t=woT_t,
            wfc_sh=np.ascontiguousarray(wfc[r * FCR:(r + 1) * FCR, :]),
            wpj_sh=np.ascontiguousarray(wpj[r * PJR:(r + 1) * PJR, :]),
            lrows=lrows, ident=ident, ones_c=ones_c, ones_r=ones_r,
            csts=csts, masks=masks, cos_t=cos_t, sin_t=sin_t,
            ones_r32=np.ones((1, 128), dtype=np.float32),
            qrj=np.tile((1.0 / SHRINKS.astype(np.float32)).astype(np.float32)[None, :],
                        (128, 1)),
            qsh=np.tile(SHRINKS.astype(np.float32), (128, 32 // 16 * 16)).reshape(128, 512),
        ))
    return in_maps


def kernel(**inputs):
    in_maps = _host_inputs(inputs)
    nc, _ = build()
    res = run_bass_kernel_spmd(nc, in_maps, core_ids=list(range(NCOR)))
    out = np.concatenate([np.asarray(res.results[r]["out"]) for r in range(NCOR)], axis=0)
    return out.reshape(1, T, C).astype(np.float32)
